# version 03
# baseline (speedup 1.0000x reference)
import numpy as np
from contextlib import ExitStack

import concourse.bass as bass
import concourse.mybir as mybir
import concourse.bass_utils as bass_utils

F32 = mybir.dt.float32
BF16 = mybir.dt.bfloat16
AX = mybir.AxisListType
OP = mybir.AluOpType
ACT = mybir.ActivationFunctionType

T_ALL = 8192
T_OWN = 1024
D = 4096
INT = 8192
LAT = 2048
NE = 8
NDC = D // 128
NST = T_ALL // 128
EPS = 1e-5
MASK_NEG = -30000.0
N_CORES = 8


class Sem:
    def __init__(self, es, nc, name):
        self.h = es.enter_context(nc.semaphore(name))
        self.n = 0


class Prog:
    def __init__(self):
        self.ops = {k: [] for k in ("sync", "scalar", "vector", "tensor", "gpsimd")}
        self.eng_prim = {}
        self.eng_state = {k: {} for k in self.ops}

    def set_engine_sems(self, m):
        self.eng_prim = m

    def op(self, eng, emit, waits=(), incs=(), dma=False):
        waits = [(s, v) for (s, v) in waits if v > 0]
        incs = list(incs)
        if eng in self.eng_prim and emit is not None and not dma:
            for s, c in self.eng_state[eng].items():
                if c > 0:
                    waits.append((s, c))
            if not incs:
                incs.append((self.eng_prim[eng], 1))
        dd = {}
        for s, v in waits:
            if id(s) not in dd or dd[id(s)][1] < v:
                dd[id(s)] = (s, v)
        waits = list(dd.values())
        for s, v in waits:
            assert v <= s.n, f"wait target {v} > scheduled count {s.n} ({eng})"
        inc_rec = []
        for s, k in incs:
            s.n += k
            inc_rec.append((s, k))
            if eng in self.eng_prim and not dma:
                st = self.eng_state[eng]
                st[s] = st.get(s, 0) + k
        self.ops[eng].append((waits, emit, inc_rec))

    def emit_engine(self, name, eng):
        for waits, emit, incs in self.ops[name]:
            for s, v in waits:
                eng.wait_ge(s.h, v)
            instr = emit(eng) if emit is not None else None
            if incs:
                if instr is None:
                    instr = eng.nop()
                for s, k in incs:
                    instr.then_inc(s.h, k)


class Stream:

    def __init__(self, es, nc, name, slots):
        self.sems = [Sem(es, nc, f"{name}{i}") for i in range(slots)]
        self.slots = slots

    def sem(self, k):
        return self.sems[k % self.slots]

    def totals(self):
        return [(s, s.n) for s in self.sems]


_LAST_PROG = None


def build_program(stop_after=99, npass=2):
    nc = bass.Bass("TRN2", target_bir_lowering=False)

    dr = lambda n, s, dt, k="ExternalInput": nc.dram_tensor(n, s, dt, kind=k).ap()
    x_own = dr("x_own", [T_OWN, D], F32)
    x_ownT = dr("x_ownT", [D, T_OWN], F32)
    x_full = dr("x_full", [T_ALL, D], F32)
    tb_in = dr("tb_in", [1, 1], F32)
    up_w = dr("up_w", [D, INT], F32)
    down_w = dr("down_w", [INT, D], F32)
    fc1_w = dr("fc1_w", [D, LAT], F32)
    fc2_w = dr("fc2_w", [LAT, D], F32)
    exp_w = dr("exp_w", [LAT, LAT], F32)
    gate_wT = dr("gate_wT", [NE, D], F32)
    gamma = dr("gamma", [D], F32)
    beta = dr("beta", [D], F32)
    bv_q = dr("bv_q", [4, 1024], F32)
    bv_k = dr("bv_k", [4, 1024], F32)
    bv_v = dr("bv_v", [4, 1024], F32)

    out_fm = dr("out_fm", [NDC, 128, T_OWN], F32, "ExternalOutput")

    xbf_scr = dr("xbf_scr", [T_ALL, D], BF16, "Internal")
    h_scr = dr("h_scr", [INT // 128, 128, T_OWN], BF16, "Internal")
    y_scr = dr("y_scr", [NDC, 128, T_OWN], F32, "Internal")
    xffn_scr = dr("xffn_scr", [NDC, 128, T_OWN], F32, "Internal")
    moe_scr = dr("moe_scr", [16, 128, T_OWN], BF16, "Internal")
    bounce_f = dr("bounce_f", [64, 128], F32, "Internal")
    bounce_bf = dr("bounce_bf", [64, 128], BF16, "Internal")
    cb_scr = dr("cb_scr", [4, 1024], F32, "Internal")
    vb_scr = dr("vb_scr", [4, 1024], F32, "Internal")

    p = Prog()
    global _LAST_PROG
    _LAST_PROG = p
    SYNC, SCAL, VEC, TEN, GPS = "sync", "scalar", "vector", "tensor", "gpsimd"

    with ExitStack() as es:
        sb = lambda n, s, dt: es.enter_context(nc.sbuf_tensor(n, s, dt))

        xn_fm = sb("xn_fm", [128, NDC, T_OWN], BF16)
        big = sb("big", [128, 32768], BF16)
        AF = sb("AF", [128, 8192], F32)
        AB = sb("AB", [128, 10240], BF16)
        stat_mu = sb("stat_mu", [128, NST], F32)
        stat_negmu_bf = sb("stat_negmu_bf", [128, NST], BF16)
        stat_negmur_bf = sb("stat_negmur_bf", [128, NST], BF16)
        stat_r = sb("stat_r", [128, NST], F32)
        stat_r8 = sb("stat_r8", [128, NST], F32)
        own_mu = sb("own_mu", [128, 8], F32)
        own_r = sb("own_r", [128, 8], F32)
        cb_fm = sb("cb_fm", [128, 32], F32)
        vb_fm = sb("vb_fm", [128, 32], F32)
        gam_fm = sb("gam_fm", [128, 32], F32)
        bet_fm = sb("bet_fm", [128, 32], F32)
        gate_fm = sb("gate_fm", [128, 32, NE], BF16)
        rw_rep = sb("rw_rep", [128, T_OWN], BF16)
        negmu_stage = sb("negmu_stage", [64, 128], BF16)
        negmu_row = sb("negmu_row", [1, 2, 512], BF16)
        qt_row = sb("qt_row", [1, T_OWN], BF16)
        qt_sb = sb("qt_sb", [1, D], BF16)
        mask01 = sb("mask01", [128, 2, 512], BF16)
        mask05a = sb("mask05a", [128, 2, 512], BF16)
        mask05b = sb("mask05b", [128, 2, 512], BF16)
        thr_tab = sb("thr_tab", [128, 128], F32)
        m_row = sb("m_row", [1, 512], BF16)
        ones_bf = sb("ones_bf", [128, 1], BF16)
        oneq_bf = sb("oneq_bf", [128, 1], BF16)
        onerow_bf = sb("onerow_bf", [1, 128], BF16)
        onerow_f = sb("onerow_f", [1, 128], F32)
        ident_f = sb("ident_f", [128, 128], F32)
        tb_col = sb("tb_col", [128, 1], F32)
        thr_col = sb("thr_col", [128, 1], F32)
        tiny = sb("tiny", [128, 16], F32)
        pack16 = sb("pack16", [128, 16], F32)
        bv_alpha = sb("bv_alpha", [4, 3], F32)
        cbvb_sb = sb("cbvb_sb", [32, 128], F32)
        stage_t16 = cbvb_sb[0:16, :]
        logit_s = sb("logit_s", [128, NE], F32)

        rs = lambda ap, spec, **kw: ap.rearrange(spec, **kw)
        bvq_sb = AF[0:4, 0:1024]
        bvk_sb = AF[0:4, 1024:2048]
        bvv_sb = AF[0:4, 2048:3072]
        bv_tmp = AF[0:4, 3072:4096]
        x_tm = rs(AF[:, 0:4096], "p (a b) -> p a b", a=2)
        xT_st = rs(AF[:, 4096:6144], "p (a b) -> p a b", a=2)
        ownr_row = AF[0:1, 6144:7168]
        ownm_row = AF[0:1, 7168:8192]
        gate_sb = AF[0:8, 0:4096]
        rw_row = AF[0:1, 7168:8192]
        w_raw = rs(AF[:, 4096:6144], "p (a b) -> p a b", a=4)
        relu_s = AF[:, 6144:7168]
        y_stage = rs(AF[:, 0:2048], "p (a b) -> p a b", a=4)
        y_rb = rs(AF[:, 2048:4096], "p (a b) -> p a b", a=4)
        y_in = rs(AF[:, 0:2048], "p (a b) -> p a b", a=2)
        t_nrm = rs(AF[:, 2048:4096], "p (a b) -> p a b", a=2)
        mu2_row = AF[0:1, 4096:5120]
        r2_row = AF[0:1, 5120:6144]
        var_row = AF[0:1, 6144:7168]
        scr_row = AF[0:1, 7168:8192]
        out_st = rs(AF[:, 0:2048], "p (a b) -> p a b", a=4)
        sig_in = rs(AF[:, 2048:3072], "p (a b) -> p a b", a=2)
        mask_t = rs(AF[:, 3072:4096], "p (a b) -> p a b", a=2)
        sg_t = rs(AF[:, 4096:5120], "p (a b) -> p a b", a=2)
        xffn_rb = rs(AF[:, 5120:7168], "p (a b) -> p a b", a=4)
        w_bf = rs(AB[:, 0:2048], "p (a b) -> p a b", a=4)
        h_stage = rs(AB[:, 2048:4096], "p (a b) -> p a b", a=2)
        moe_stage = rs(AB[:, 2048:4096], "p (a b) -> p a b", a=2)
        xbf_st = rs(AB[:, 4096:8192], "p (a b) -> p a b", a=2)
        ybf = rs(AB[:, 2048:4096], "p (a b) -> p a b", a=2)
        ysq = rs(AB[:, 4096:6144], "p (a b) -> p a b", a=2)
        Kp = rs(AB[:, 0:8192], "p (a c b) -> p a c b", a=2, c=8)
        diff_t = AF[:, 7168:7680]
        v_t = rs(AB[:, 8192:10240], "p (a b) -> p a b", a=2)
        moe_rb = rs(AB[:, 8192:10240], "p (a b) -> p a b", a=2)
        lat_v = rs(big[:, 0:16384], "p (a b) -> p a b", a=16)

        ps = [es.enter_context(nc.psum_tensor(f"ps{i}", [128, 512], F32)) for i in range(8)]

        sm = lambda n: Sem(es, nc, n)
        sV, sS, sT, sWC, sG = sm("sV"), sm("sS"), sm("sT"), sm("sWC"), sm("sG")
        p.set_engine_sems({"vector": sV, "scalar": sS, "gpsimd": sG})
        st_ = lambda n, k: Stream(es, nc, n, k)
        stb = st_("stb", 1)
        sbv = st_("sbv", 1)
        scv = st_("scv", 2)
        sbn = st_("sbn", 1)
        swt = st_("swt", 4)
        sgt = st_("sgt", 1)
        sxt = st_("sxt", 2)
        sxT = st_("sxT", 2)
        sxb = st_("sxb", 2)
        shw = st_("shw", 2)
        smw = st_("smw", 2)
        shl = st_("shl", 1)
        sml = st_("sml", 2)
        skp = st_("skp", 2)
        svl = st_("svl", 2)
        syw = st_("syw", 4)
        syr = st_("syr", 4)
        sy5 = st_("sy5", 2)
        sff = st_("sff", 2)
        sfr = st_("sfr", 4)
        snm = st_("snm", 2)
        sou = st_("sou", 4)
        all_dma_streams = [stb, sbv, scv, sbn, swt, sgt, sxt, sxT, sxb, shw, smw, shl,
                           sml, skp, svl, syw, syr, sy5, sff, sfr, snm, sou]

        def finalize():
            waits = [(sV, sV.n), (sS, sS.n), (sT, sT.n), (sWC, sWC.n), (sG, sG.n)]
            for stq in all_dma_streams:
                waits += stq.totals()
            p.op(SYNC, None, waits=waits)
            with nc.Block() as block:
                block.sync(lambda e: p.emit_engine("sync", e))
                block.scalar(lambda e: p.emit_engine("scalar", e))
                block.vector(lambda e: p.emit_engine("vector", e))
                block.tensor(lambda e: p.emit_engine("tensor", e))
                block.gpsimd(lambda e: p.emit_engine("gpsimd", e))
            return nc

        def dma(q, out, in_, sem, waits=()):
            p.op(q, (lambda o, i: (lambda e: e.dma_start(out=o, in_=i)))(out, in_),
                 waits=waits, incs=[(sem, 16)], dma=True)
            return (sem, sem.n)

        def dmaT(q, out, in_, sem, waits=()):
            p.op(q, (lambda o, i: (lambda e: e.dma_start_transpose(out=o, in_=i)))(out, in_),
                 waits=waits, incs=[(sem, 16)], dma=True)
            return (sem, sem.n)

        p.op(GPS, lambda e: e.iota(ident_f[:], pattern=[[-1, 128]], base=0,
                                   channel_multiplier=1,
                                   allow_small_or_imprecise_dtypes=True), incs=[(sG, 1)])
        gi = sG.n
        p.op(VEC, lambda e: e.tensor_scalar(ident_f[:], ident_f[:], 0.0, None, OP.is_equal),
             waits=[(sG, gi)], incs=[(sV, 1)])
        ident_done = sV.n
        p.op(VEC, lambda e: e.memset(ones_bf[:], 1.0), incs=[(sV, 1)])
        p.op(VEC, lambda e: e.memset(oneq_bf[:], 1.0 / 4096.0), incs=[(sV, 1)])
        p.op(VEC, lambda e: e.memset(onerow_bf[:], 1.0), incs=[(sV, 1)])
        p.op(VEC, lambda e: e.memset(onerow_f[:], 1.0), incs=[(sV, 1)])
        consts_memset = sV.n

        tb_ev = dma(SCAL, tiny[0:1, 0:1], tb_in[:], stb.sem(0))
        p.op(TEN, lambda e: e.matmul(ps[3][0:128, 0:1], onerow_f[:], tiny[0:1, 0:1],
                                     start=True, stop=True),
             waits=[tb_ev, (sV, consts_memset)], incs=[(sT, 1)])
        p.op(VEC, lambda e: e.tensor_copy(tb_col[:], ps[3][0:128, 0:1]),
             waits=[(sT, sT.n)], incs=[(sV, 1)])
        tb_done = sV.n

        dma(SCAL, bvq_sb, bv_q[:], sbv.sem(0))
        dma(SCAL, bvk_sb, bv_k[:], sbv.sem(0))
        bv_ev = dma(SCAL, bvv_sb, bv_v[:], sbv.sem(0))

        def bind(src, col):
            a = bv_alpha[:, col:col + 1]
            p.op(VEC, (lambda s_, a_: lambda e: e.reduce_sum(
                a_, s_, axis=AX.X, apply_absolute_value=True))(src, a),
                 waits=[bv_ev], incs=[(sV, 1)])
            p.op(VEC, (lambda a_: lambda e: e.tensor_scalar_mul(a_, a_, 1.0 / 1024.0))(a),
                 incs=[(sV, 1)])
            vv = sV.n
            p.op(SCAL, (lambda s_: lambda e: e.sign(bv_tmp, s_))(src),
                 waits=[(sV, vv)], incs=[(sS, 1)])
            ss = sS.n
            p.op(VEC, (lambda a_: lambda e: e.tensor_scalar_mul(bv_tmp, bv_tmp, a_))(a),
                 waits=[(sS, ss)], incs=[(sV, 1)])
            p.op(VEC, (lambda s_: lambda e: e.tensor_scalar(s_, bv_tmp, 0.0, None,
                                                            OP.is_equal))(src), incs=[(sV, 1)])
            p.op(VEC, (lambda s_, a_: lambda e: e.scalar_tensor_tensor(
                s_, s_, a_, bv_tmp, OP.mult, OP.add))(src, a), incs=[(sV, 1)])

        bind(bvq_sb, 0)
        bind(bvk_sb, 1)
        bind(bvv_sb, 2)
        p.op(VEC, lambda e: e.tensor_mul(bvq_sb, bvq_sb, bvk_sb), incs=[(sV, 1)])
        binds_done = sV.n

        wcb = dma(SCAL, cb_scr, bvq_sb, scv.sems[0], waits=[(sV, binds_done)])
        wvb = dma(SCAL, vb_scr, bvv_sb, scv.sems[1])

        def to_fm(dst, src_ap, slot, extra, prev_copy):
            ld = dma(SCAL, cbvb_sb[:], src_ap, scv.sems[slot],
                     waits=list(extra) + ([(sV, prev_copy)] if prev_copy else []))
            p.op(TEN, lambda e: e.transpose(ps[0][:, 0:32], cbvb_sb[:], ident_f[0:32, 0:32]),
                 waits=[ld, (sV, ident_done)], incs=[(sT, 1)])
            tt = sT.n
            p.op(VEC, (lambda d: lambda e: e.tensor_copy(d[:], ps[0][:, 0:32]))(dst),
                 waits=[(sT, tt)], incs=[(sV, 1)])
            return sV.n

        e1 = to_fm(cb_fm, cb_scr.rearrange("h (c p) -> (h c) p", p=128), 0, [wcb], 0)
        e2 = to_fm(vb_fm, vb_scr.rearrange("h (c p) -> (h c) p", p=128), 1, [wvb], e1)
        e3 = to_fm(gam_fm, gamma.rearrange("(c p) -> c p", p=128), 0, [], e2)
        consts_done = to_fm(bet_fm, beta.rearrange("(c p) -> c p", p=128), 1, [], e3)

        def rsqrt_nr(dst, var, t1, t2):
            p.op(VEC, (lambda a, b: lambda e: e.tensor_scalar_add(a, b, EPS))(t1, var),
                 incs=[(sV, 1)])
            vv = sV.n
            p.op(SCAL, (lambda a, b: lambda e: e.sqrt(a, b))(t2, t1),
                 waits=[(sV, vv)], incs=[(sS, 1)])
            ss = sS.n
            p.op(VEC, (lambda a, b: lambda e: e.reciprocal(a, b))(dst, t2),
                 waits=[(sS, ss)], incs=[(sV, 1)])
            p.op(VEC, (lambda a, b: lambda e: e.tensor_mul(a, b, b))(t2, dst), incs=[(sV, 1)])
            p.op(VEC, (lambda a, b: lambda e: e.tensor_mul(a, a, b))(t2, t1), incs=[(sV, 1)])
            p.op(VEC, (lambda a: lambda e: e.tensor_scalar(a, a, -0.5, 1.5, OP.mult, OP.add))(t2),
                 incs=[(sV, 1)])
            p.op(VEC, (lambda a, b: lambda e: e.tensor_mul(a, a, b))(dst, t2), incs=[(sV, 1)])

        xbf_cnt = [0, 0]

        def stats_tile(idx, src_hbm, mu_dst, r_dst, c, do_cast):
            for hh in range(2):
                buf = x_tm[:, hh]
                ld = dma(SCAL, buf, src_hbm[idx * 128:(idx + 1) * 128,
                                            hh * 2048:(hh + 1) * 2048], sxt.sems[hh])
                p.op(VEC, (lambda b, col: lambda e: e.reduce_sum(col, b, axis=AX.X))(
                    buf, tiny[:, c + hh:c + hh + 1]), waits=[ld], incs=[(sV, 1)])
                if do_cast:
                    g = []
                    if xbf_cnt[hh] > 0:
                        g.append((sxb.sems[hh], 16 * xbf_cnt[hh]))
                    p.op(VEC, (lambda b, hh_: lambda e: e.tensor_copy(
                        xbf_st[:, hh_], b))(buf, hh), waits=g, incs=[(sV, 1)])
                cv = sV.n
                p.op(SCAL, (lambda b, col: lambda e: e.activation(
                    b, b, ACT.Square, accum_out=col))(buf, tiny[:, c + 2 + hh:c + 3 + hh]),
                     waits=[(sV, cv)], incs=[(sS, 1)])
                if do_cast:
                    dma(SCAL, xbf_scr[idx * 128:(idx + 1) * 128, hh * 2048:(hh + 1) * 2048],
                        xbf_st[:, hh], sxb.sems[hh], waits=[(sV, cv)])
                    xbf_cnt[hh] += 1
            sq_s = sS.n
            p.op(VEC, (lambda m: lambda e: e.tensor_add(m, tiny[:, c:c + 1],
                                                        tiny[:, c + 1:c + 2]))(mu_dst),
                 incs=[(sV, 1)])
            p.op(VEC, (lambda m: lambda e: e.tensor_scalar_mul(m, m, 1.0 / D))(mu_dst),
                 incs=[(sV, 1)])
            p.op(VEC, lambda e: e.tensor_add(tiny[:, c + 2:c + 3], tiny[:, c + 2:c + 3],
                                             tiny[:, c + 3:c + 4]),
                 waits=[(sS, sq_s)], incs=[(sV, 1)])
            p.op(VEC, lambda e: e.tensor_scalar_mul(tiny[:, c + 2:c + 3],
                                                    tiny[:, c + 2:c + 3], 1.0 / D),
                 incs=[(sV, 1)])
            p.op(VEC, (lambda m: lambda e: e.tensor_mul(tiny[:, c + 4:c + 5], m, m))(mu_dst),
                 incs=[(sV, 1)])
            p.op(VEC, lambda e: e.tensor_sub(tiny[:, c + 4:c + 5], tiny[:, c + 2:c + 3],
                                             tiny[:, c + 4:c + 5]), incs=[(sV, 1)])
            rsqrt_nr(r_dst, tiny[:, c + 4:c + 5], tiny[:, c + 5:c + 6], tiny[:, c + 2:c + 3])
            return sV.n, sS.n

        own_evt = []
        for i in range(8):
            if i == 0:
                p.op(SCAL, None, waits=[wcb, wvb, (sV, e2)])
            else:
                p.op(SCAL, None, waits=[(sV, own_evt[i - 1][0])])
            own_evt.append(stats_tile(i, x_own, own_mu[:, i:i + 1], own_r[:, i:i + 1],
                                      0, False))

        p.op(VEC, lambda e: e.tensor_copy(pack16[:, 0:8], own_r[:]), incs=[(sV, 1)])
        p.op(VEC, lambda e: e.tensor_mul(pack16[:, 8:16], own_mu[:], own_r[:]), incs=[(sV, 1)])
        p.op(VEC, lambda e: e.tensor_scalar_mul(pack16[:, 8:16], pack16[:, 8:16], -1.0),
             incs=[(sV, 1)])
        pk = sV.n
        p.op(TEN, lambda e: e.transpose(ps[1][0:16, 0:128], pack16[:], ident_f[:]),
             waits=[(sV, pk)], incs=[(sT, 1)])
        tt = sT.n
        p.op(VEC, lambda e: e.tensor_copy(stage_t16, ps[1][0:16, 0:128]),
             waits=[(sT, tt), (sV, consts_done)], incs=[(sV, 1)])
        st16 = sV.n
        bw = dma(SCAL, bounce_f[0:16, :], stage_t16, sbn.sems[0], waits=[(sV, st16)])
        br1 = dma(SCAL, ownr_row, bounce_f[0:8, :].rearrange("a b -> (a b)")[None, :],
                  sbn.sems[0], waits=[bw])
        br2 = dma(SCAL, ownm_row, bounce_f[8:16, :].rearrange("a b -> (a b)")[None, :],
                  sbn.sems[0], waits=[br1])

        for hh in range(2):
            p.op(TEN, (lambda h_: lambda e: e.matmul(
                ps[4 + h_][:], onerow_f[:], ownr_row[0:1, h_ * 512:(h_ + 1) * 512],
                start=True, stop=True))(hh),
                 waits=[br2] if hh == 0 else [], incs=[(sT, 1)])
            p.op(TEN, (lambda h_: lambda e: e.matmul(
                ps[6 + h_][:], onerow_f[:], ownm_row[0:1, h_ * 512:(h_ + 1) * 512],
                start=True, stop=True))(hh), incs=[(sT, 1)])
        reps0_mm = sT.n

        xn_evt = []
        for k in range(NDC):
            buf = xT_st[:, k % 2]
            w = [(sV, xn_evt[k - 2])] if k >= 2 else [(sV, own_evt[7][0])]
            ld = dma(SCAL, buf, x_ownT[k * 128:(k + 1) * 128, :], sxT.sems[k % 2], waits=w)
            for hh in range(2):
                p.op(VEC, (lambda b, h_: lambda e: e.tensor_mul(
                    b[:, h_ * 512:(h_ + 1) * 512], b[:, h_ * 512:(h_ + 1) * 512],
                    ps[4 + h_][:]))(buf, hh),
                     waits=[ld, (sT, reps0_mm)], incs=[(sV, 1)])
                p.op(VEC, (lambda kk, b, h_: lambda e: e.tensor_add(
                    xn_fm[:, kk, h_ * 512:(h_ + 1) * 512],
                    b[:, h_ * 512:(h_ + 1) * 512], ps[6 + h_][:]))(k, buf, hh),
                     incs=[(sV, 1)])
            xn_evt.append(sV.n)
        xn_ready = sV.n

        if stop_after <= 0:
            return finalize()

        gate_ev = dma(SYNC, gate_sb, gate_wT[:], sgt.sem(0),
                      waits=[(sV, own_evt[7][0]), (sS, own_evt[7][1])])
        gfm_evt = 0
        for k in range(NDC):
            w = [gate_ev, (sV, ident_done)]
            if gfm_evt:
                w.append((sV, gfm_evt))
            p.op(TEN, (lambda kk: lambda e: e.transpose(
                ps[2][0:128, 0:8], gate_sb[0:8, kk * 128:(kk + 1) * 128],
                ident_f[0:8, 0:8]))(k),
                 waits=w, incs=[(sT, 1)])
            tt = sT.n
            p.op(VEC, (lambda kk: lambda e: e.tensor_copy(gate_fm[:, kk], ps[2][0:128, 0:8]))(k),
                 waits=[(sT, tt)], incs=[(sV, 1)])
            gfm_evt = sV.n

        l_m1, l_m2, l_cnt, l_two = (tiny[:, 8:9], tiny[:, 9:10], tiny[:, 10:11], tiny[:, 11:12])
        top2_evt = 0
        for tc in range(8):
            for dc in range(NDC):
                w = []
                if tc == 0 and dc == 0:
                    w = [(sV, xn_ready), (sV, gfm_evt), (sV, tb_done)]
                if dc == 0 and tc > 0:
                    w.append((sV, top2_evt))
                p.op(TEN, (lambda t_, d_: lambda e: e.matmul(
                    ps[3][0:128, 0:8], xn_fm[:, d_, t_ * 128:(t_ + 1) * 128], gate_fm[:, d_],
                    start=(d_ == 0), stop=(d_ == NDC - 1)))(tc, dc),
                     waits=w, incs=([(sT, 1)] if dc == NDC - 1 else []))
            mm = sT.n
            lgp = ps[3][0:128, 0:8]
            p.op(VEC, lambda e: e.reduce_max(l_m1, lgp, axis=AX.X), waits=[(sT, mm)],
                 incs=[(sV, 1)])
            p.op(VEC, lambda e: e.tensor_scalar(logit_s[:], lgp, l_m1, None, OP.is_ge),
                 incs=[(sV, 1)])
            p.op(VEC, lambda e: e.reduce_sum(l_cnt, logit_s[:], axis=AX.X), incs=[(sV, 1)])
            p.op(VEC, lambda e: e.scalar_tensor_tensor(logit_s[:], logit_s[:], -1e30, lgp,
                                                       OP.mult, OP.add), incs=[(sV, 1)])
            p.op(VEC, lambda e: e.reduce_max(l_m2, logit_s[:], axis=AX.X), incs=[(sV, 1)])
            p.op(VEC, lambda e: e.tensor_scalar(l_two, l_cnt, 1.5, None, OP.is_ge), incs=[(sV, 1)])
            p.op(VEC, lambda e: e.tensor_sub(l_cnt, l_m1, l_m2), incs=[(sV, 1)])
            p.op(VEC, lambda e: e.tensor_mul(l_cnt, l_cnt, l_two), incs=[(sV, 1)])
            p.op(VEC, lambda e: e.tensor_add(l_cnt, l_cnt, l_m1), incs=[(sV, 1)])
            p.op(VEC, (lambda t_: lambda e: e.tensor_add(pack16[:, t_:t_ + 1], l_cnt, l_m2))(tc),
                 incs=[(sV, 1)])
            top2_evt = sV.n

        p.op(TEN, lambda e: e.transpose(ps[2][0:8, 0:128], pack16[:, 0:8], ident_f[:]),
             waits=[(sV, top2_evt)], incs=[(sT, 1)])
        tt = sT.n
        p.op(VEC, lambda e: e.tensor_copy(stage_t16[0:8, :], ps[2][0:8, 0:128]),
             waits=[(sT, tt)], incs=[(sV, 1)])
        rwv = sV.n
        bw = dma(SCAL, bounce_f[0:8, :], stage_t16[0:8, :], sbn.sems[0],
                 waits=[(sV, rwv), br2])
        br = dma(SCAL, rw_row, bounce_f[0:8, :].rearrange("a b -> (a b)")[None, :],
                 sbn.sems[0], waits=[bw, (sT, reps0_mm)])
        p.op(VEC, lambda e: e.tensor_copy(qt_row[:], rw_row), waits=[br], incs=[(sV, 1)])
        qtc = sV.n
        for hh in range(2):
            p.op(TEN, (lambda h_: lambda e: e.matmul(
                ps[2 + h_][:], onerow_bf[:], qt_row[0:1, h_ * 512:(h_ + 1) * 512],
                start=True, stop=True))(hh),
                 waits=[(sV, qtc)] if hh == 0 else [], incs=[(sT, 1)])
            p.op(VEC, (lambda h_: lambda e: e.tensor_copy(
                rw_rep[:, h_ * 512:(h_ + 1) * 512], ps[2 + h_][:]))(hh),
                 waits=[(sT, sT.n)], incs=[(sV, 1)])
        rw_done = sV.n

        wcast_evt = {}
        mm_evt_of_tile = {}
        wi_box = {"i": 0}

        def wtile(src_ap, extra_dma_waits=()):
            i = wi_box["i"]
            wi_box["i"] = i + 1
            wd = list(extra_dma_waits)
            if i >= 4:
                wd.append((sWC, wcast_evt[i - 4]))
            ld = dma(SYNC, w_raw[:, i % 4], src_ap, swt.sems[i % 4], waits=wd)
            wc = [ld]
            if i >= 4:
                wc.append((sT, mm_evt_of_tile[i - 4]))
            p.op(SCAL, (lambda b_: lambda e: e.copy(w_bf[:, b_], w_raw[:, b_]))(i % 4),
                 waits=wc, incs=[(sWC, 1)])
            wcast_evt[i] = sWC.n
            return i, sWC.n


        fc1_evac_evt = 0
        for lgr in range(4):
            for dc in range(NDC):
                ew = [(sV, xn_ready), (sV, rw_done)] if (lgr == 0 and dc == 0) else []
                i, ce = wtile(fc1_w[dc * 128:(dc + 1) * 128, lgr * 512:(lgr + 1) * 512], ew)
                for sub in range(4):
                    for hf in range(2):
                        w = []
                        if sub == 0 and hf == 0:
                            w.append((sWC, ce))
                            if dc == 0:
                                w.append((sV, xn_ready))
                                w.append((sV, rw_done))
                                if lgr > 0:
                                    w.append((sV, fc1_evac_evt))
                        p.op(TEN, (lambda b_, s_, h_, d_: lambda e: e.matmul(
                            ps[s_ * 2 + h_][:], w_bf[:, b_, s_ * 128:(s_ + 1) * 128],
                            xn_fm[:, d_, h_ * 512:(h_ + 1) * 512],
                            start=(d_ == 0), stop=(d_ == NDC - 1)))(i % 4, sub, hf, dc),
                             waits=w,
                             incs=([(sT, 1)] if (sub == 3 and hf == 1) else []))
                mm_evt_of_tile[i] = sT.n
            mm = sT.n
            for sub in range(4):
                for hf in range(2):
                    p.op(VEC, (lambda l_, s_, h_: lambda e: e.tensor_copy(
                        lat_v[:, l_ * 4 + s_, h_ * 512:(h_ + 1) * 512],
                        ps[s_ * 2 + h_][:]))(lgr, sub, hf),
                         waits=[(sT, mm)], incs=[(sV, 1)])
            fc1_evac_evt = sV.n

        if stop_after <= 1:
            return finalize()

        pro = {"i": 0, "evts": {}}

        def prologue_tiles(n):
            for _ in range(n):
                i = pro["i"]
                if i >= NST:
                    return
                pro["i"] = i + 1
                if i == 0:
                    p.op(SCAL, None, waits=[(sV, gfm_evt), (sV, own_evt[7][0])])
                else:
                    p.op(SCAL, None, waits=[(sV, pro["evts"][i - 1][0])])
                ev = stats_tile(i, x_full, stat_mu[:, i:i + 1], stat_r[:, i:i + 1], 0, True)
                p.op(VEC, (lambda i_: lambda e: e.tensor_scalar_mul(
                    stat_r8[:, i_:i_ + 1], stat_r[:, i_:i_ + 1], 0.125))(i), incs=[(sV, 1)])
                p.op(VEC, (lambda i_: lambda e: e.tensor_scalar_mul(
                    stat_negmu_bf[:, i_:i_ + 1], stat_mu[:, i_:i_ + 1], -1.0))(i),
                     incs=[(sV, 1)])
                p.op(VEC, (lambda i_: lambda e: e.tensor_mul(
                    tiny[:, 14:15], stat_mu[:, i_:i_ + 1], stat_r[:, i_:i_ + 1]))(i),
                     incs=[(sV, 1)])
                p.op(VEC, (lambda i_: lambda e: e.tensor_scalar_mul(
                    stat_negmur_bf[:, i_:i_ + 1], tiny[:, 14:15], -1.0))(i),
                     incs=[(sV, 1)])
                pro["evts"][i] = (sV.n, ev[1])

        NFG = INT // 512
        up_relu_evt = 0
        sq_evts = {}
        hw_cnt = [0, 0]
        for fg in range(NFG):
            for dc in range(NDC):
                ew = [(sV, xn_ready)] if (fg == 0 and dc == 0) else []
                i, ce = wtile(up_w[dc * 128:(dc + 1) * 128, fg * 512:(fg + 1) * 512], ew)
                for sub in range(4):
                    for hf in range(2):
                        w = []
                        if sub == 0 and hf == 0:
                            w.append((sWC, ce))
                            if dc == 0:
                                if fg == 0:
                                    w.append((sV, fc1_evac_evt))
                                else:
                                    w.append((sS, up_relu_evt))
                        p.op(TEN, (lambda b_, s_, h_, d_: lambda e: e.matmul(
                            ps[s_ * 2 + h_][:], w_bf[:, b_, s_ * 128:(s_ + 1) * 128],
                            xn_fm[:, d_, h_ * 512:(h_ + 1) * 512],
                            start=(d_ == 0), stop=(d_ == NDC - 1)))(i % 4, sub, hf, dc),
                             waits=w,
                             incs=([(sT, 1)] if (sub == 3 and hf == 1) else []))
                mm_evt_of_tile[i] = sT.n
            mm = sT.n
            for sub in range(4):
                j = fg * 4 + sub
                for hf in range(2):
                    w = [(sT, mm)]
                    if hf == 0 and j >= 1:
                        w.append((sV, sq_evts[j - 1]))
                    p.op(SCAL, (lambda s_, h_: lambda e: e.activation(
                        relu_s[:, h_ * 512:(h_ + 1) * 512],
                        ps[s_ * 2 + h_][:], ACT.Relu))(sub, hf),
                         waits=w, incs=[(sS, 1)])
                rl = sS.n
                w = [(sS, rl)]
                if hw_cnt[j % 2] > 0:
                    w.append((shw.sems[j % 2], 16 * hw_cnt[j % 2]))
                p.op(VEC, (lambda j_: lambda e: e.tensor_mul(
                    h_stage[:, j_ % 2], relu_s[:], relu_s[:]))(j),
                     waits=w, incs=[(sV, 1)])
                sq_evts[j] = sV.n
                dma(SCAL, h_scr[j], h_stage[:, j % 2], shw.sems[j % 2],
                    waits=[(sV, sq_evts[j])])
                hw_cnt[j % 2] += 1
            up_relu_evt = sS.n
            prologue_tiles(4)
        prologue_tiles(NST)
        stats_done_v = sV.n
        up_done_s = sS.n
        h_all = shw.totals()
        xbf_all = sxb.totals()

        p.op(TEN, lambda e: e.transpose(ps[4][0:64, 0:128], stat_mu[:], ident_f[:]),
             waits=[(sV, stats_done_v), (sS, up_done_s)], incs=[(sT, 1)])
        tt = sT.n
        p.op(VEC, lambda e: e.tensor_scalar_mul(negmu_stage[:], ps[4][0:64, 0:128], -1.0),
             waits=[(sT, tt)], incs=[(sV, 1)])
        negmu_psum_free = sV.n
        negmu_wr = dma(SCAL, bounce_bf[:], negmu_stage[:], sbn.sems[0],
                       waits=[(sV, negmu_psum_free), br])

        if stop_after <= 2:
            return finalize()

        exp_evac_evt = 0
        smo_guard = {}
        mw_cnt = [0, 0]
        for mg in range(4):
            for lc in range(16):
                i, ce = wtile(exp_w[lc * 128:(lc + 1) * 128, mg * 512:(mg + 1) * 512])
                for sub in range(4):
                    for hf in range(2):
                        w = []
                        if sub == 0 and hf == 0:
                            w.append((sWC, ce))
                            if lc == 0:
                                if mg == 0:
                                    w.append((sS, up_done_s))
                                    w.append((sV, negmu_psum_free))
                                else:
                                    w.append((sV, exp_evac_evt))
                        p.op(TEN, (lambda b_, s_, h_, l_: lambda e: e.matmul(
                            ps[s_ * 2 + h_][:], w_bf[:, b_, s_ * 128:(s_ + 1) * 128],
                            lat_v[:, l_, h_ * 512:(h_ + 1) * 512],
                            start=(l_ == 0), stop=(l_ == 15)))(i % 4, sub, hf, lc),
                             waits=w,
                             incs=([(sT, 1)] if (sub == 3 and hf == 1) else []))
                mm_evt_of_tile[i] = sT.n
            mm = sT.n
            for sub in range(4):
                m_chunk = mg * 4 + sub
                for hf in range(2):
                    w = [(sT, mm)]
                    if mg == 0 and sub == 0 and hf == 0:
                        w.append((sV, rw_done))
                        w += h_all
                    if mw_cnt[m_chunk % 2] > 0 and hf == 0:
                        w.append((smw.sems[m_chunk % 2], 16 * mw_cnt[m_chunk % 2]))
                    p.op(VEC, (lambda m_, s_, h_: lambda e: e.tensor_mul(
                        moe_stage[:, m_ % 2, h_ * 512:(h_ + 1) * 512], ps[s_ * 2 + h_][:],
                        rw_rep[:, h_ * 512:(h_ + 1) * 512]))(m_chunk, sub, hf),
                         waits=w, incs=[(sV, 1)])
                smo_guard[m_chunk] = dma(SCAL, moe_scr[m_chunk], moe_stage[:, m_chunk % 2],
                                         smw.sems[m_chunk % 2], waits=[(sV, sV.n)])
                mw_cnt[m_chunk % 2] += 1
            exp_evac_evt = sV.n
        moe_all = smw.totals()
        exp_last_mm = sT.n

        if stop_after <= 3:
            return finalize()

        y_evac_evt = exp_evac_evt
        big_last_mm = 0
        hl_cnt = 0
        ywr_cnt = [0, 0, 0, 0]
        yadd1 = []
        yrd_cnt = [0, 0, 0, 0]
        ml_cnt = [0, 0]
        fc2_mm = {}
        y_half0_all = []
        for half in range(2):
            w = list(moe_all) + list(h_all) + [(sT, exp_last_mm)]
            if half == 1:
                w = [(sT, big_last_mm)]
            for fl in range(32):
                f_chunk = half * 32 + fl
                dma(SYNC, big[:, fl * 1024:(fl + 1) * 1024], h_scr[f_chunk], shl.sems[0],
                    waits=(w if fl == 0 else []))
                hl_cnt += 1
            hload_ev = (shl.sems[0], 16 * hl_cnt)
            for g in range(8):
                for fl in range(32):
                    f_chunk = half * 32 + fl
                    i, ce = wtile(down_w[f_chunk * 128:(f_chunk + 1) * 128,
                                         g * 512:(g + 1) * 512])
                    for sub in range(4):
                        for hf in range(2):
                            w2 = []
                            if sub == 0 and hf == 0:
                                w2.append((sWC, ce))
                                if fl == 0:
                                    w2.append(hload_ev)
                                    w2.append((sV, y_evac_evt))
                            p.op(TEN, (lambda b_, s_, h_, fl_, last_: lambda e: e.matmul(
                                ps[s_ * 2 + h_][:], w_bf[:, b_, s_ * 128:(s_ + 1) * 128],
                                big[:, fl_ * 1024 + h_ * 512: fl_ * 1024 + (h_ + 1) * 512],
                                start=(fl_ == 0), stop=last_))(i % 4, sub, hf, fl,
                                                               half == 0 and fl == 31),
                                 waits=w2,
                                 incs=([(sT, 1)] if (sub == 3 and hf == 1) else []))
                    mm_evt_of_tile[i] = sT.n
                if half == 1:
                    for mc in range(16):
                        nm = g * 16 + mc
                        wv = list(moe_all) if nm == 0 else []
                        if nm >= 2:
                            wv.append((sT, fc2_mm[nm - 2]))
                        mld = dma(SYNC, moe_rb[:, nm % 2], moe_scr[mc],
                                  sml.sems[nm % 2], waits=wv)
                        ml_cnt[nm % 2] += 1
                        i, ce = wtile(fc2_w[mc * 128:(mc + 1) * 128, g * 512:(g + 1) * 512])
                        for sub in range(4):
                            for hf in range(2):
                                w2 = []
                                if sub == 0 and hf == 0:
                                    w2 = [(sWC, ce), mld]
                                p.op(TEN, (lambda b_, s_, h_, n_, last_: lambda e: e.matmul(
                                    ps[s_ * 2 + h_][:], w_bf[:, b_, s_ * 128:(s_ + 1) * 128],
                                    moe_rb[:, n_ % 2, h_ * 512:(h_ + 1) * 512],
                                    start=False, stop=last_))(i % 4, sub, hf, nm, mc == 15),
                                     waits=w2,
                                     incs=([(sT, 1)] if (sub == 3 and hf == 1) else []))
                        mm_evt_of_tile[i] = sT.n
                        fc2_mm[nm] = mm_evt_of_tile[i]
                mm = sT.n
                for sub in range(4):
                    dout_chunk = g * 4 + sub
                    for hf in range(2):
                        bank = sub * 2 + hf
                        if half == 0:
                            n = sum(ywr_cnt)
                            sl = n % 4
                            w2 = [(sT, mm)]
                            if ywr_cnt[sl] > 0:
                                w2.append((syw.sems[sl], 16 * ywr_cnt[sl]))
                            p.op(VEC, (lambda bk, n_: lambda e: e.tensor_copy(
                                y_stage[:, n_ % 4], ps[bk][:]))(bank, n),
                                 waits=w2, incs=[(sV, 1)])
                            dma(SCAL, y_scr[dout_chunk][:, hf * 512:(hf + 1) * 512],
                                y_stage[:, n % 4], syw.sems[sl], waits=[(sV, sV.n)])
                            ywr_cnt[sl] += 1
                        else:
                            n = len(yadd1)
                            sl = n % 4
                            w2 = list(y_half0_all)
                            if n >= 4:
                                w2.append((sV, yadd1[n - 4]))
                            ld = dma(SCAL, y_rb[:, n % 4],
                                     y_scr[dout_chunk][:, hf * 512:(hf + 1) * 512],
                                     syr.sems[sl], waits=w2)
                            yrd_cnt[sl] += 1
                            w3 = [(sT, mm), ld]
                            if ywr_cnt[sl] > 0:
                                w3.append((syw.sems[sl], 16 * ywr_cnt[sl]))
                            p.op(VEC, (lambda bk, n_: lambda e: e.tensor_add(
                                y_stage[:, n_ % 4], ps[bk][:], y_rb[:, n_ % 4]))(bank, n),
                                 waits=w3, incs=[(sV, 1)])
                            yadd1.append(sV.n)
                            dma(SCAL, y_scr[dout_chunk][:, hf * 512:(hf + 1) * 512],
                                y_stage[:, n % 4], syw.sems[sl], waits=[(sV, sV.n)])
                            ywr_cnt[sl] += 1
                y_evac_evt = sV.n
            if half == 0:
                big_last_mm = sT.n
                y_half0_all = syw.totals()
        y_all = syw.totals()
        big_free_evt = sT.n
        last_wcast = sWC.n

        if stop_after <= 4:
            return finalize()

        st_evts = {}
        for k in range(NDC):
            w = list(y_all)
            if k >= 2:
                w += [(sV, st_evts[k - 2][0]), (sS, st_evts[k - 2][1])]
            ld = dma(SCAL, y_in[:, k % 2], y_scr[k], sy5.sems[k % 2], waits=w)
            w_ybf = [ld]
            if k == 0:
                w_ybf += [(sV, y_evac_evt)] + list(moe_all) + list(h_all)
            if k >= 2:
                w_ybf.append((sT, st_evts[k - 2][2]))
            p.op(VEC, (lambda k_: lambda e: e.tensor_copy(ybf[:, k_ % 2], y_in[:, k_ % 2]))(k),
                 waits=w_ybf, incs=[(sV, 1)])
            cb_ = sV.n
            w_ysq = [ld]
            if k == 0:
                w_ysq += [(sWC, last_wcast)] + list(xbf_all)
            if k >= 2:
                w_ysq.append((sT, st_evts[k - 2][2]))
            p.op(SCAL, (lambda k_: lambda e: e.activation(ysq[:, k_ % 2], y_in[:, k_ % 2],
                                                          ACT.Square))(k),
                 waits=w_ysq, incs=[(sS, 1)])
            sq_ = sS.n
            for hf in range(2):
                pm = ps[0] if hf == 0 else ps[5]
                p.op(TEN, (lambda k_, h_, pm_: lambda e: e.matmul(
                    pm_[0:1, :], oneq_bf[:], ybf[:, k_ % 2, h_ * 512:(h_ + 1) * 512],
                    start=(k_ == 0), stop=(k_ == NDC - 1)))(k, hf, pm),
                     waits=([(sV, cb_)] if hf == 0 else []), incs=[])
            for hf in range(2):
                pq = ps[1] if hf == 0 else ps[6]
                p.op(TEN, (lambda k_, h_, pq_: lambda e: e.matmul(
                    pq_[0:1, :], oneq_bf[:], ysq[:, k_ % 2, h_ * 512:(h_ + 1) * 512],
                    start=(k_ == 0), stop=(k_ == NDC - 1)))(k, hf, pq),
                     waits=([(sS, sq_)] if hf == 0 else []),
                     incs=([(sT, 1)] if hf == 1 else []))
            st_evts[k] = (sV.n, sS.n, sT.n)
        mm2 = sT.n
        p.op(VEC, lambda e: e.tensor_copy(mu2_row[0:1, 0:512], ps[0][0:1, :]),
             waits=[(sT, mm2)], incs=[(sV, 1)])
        p.op(VEC, lambda e: e.tensor_copy(mu2_row[0:1, 512:1024], ps[5][0:1, :]), incs=[(sV, 1)])
        p.op(VEC, lambda e: e.tensor_mul(scr_row, mu2_row, mu2_row), incs=[(sV, 1)])
        p.op(VEC, lambda e: e.tensor_copy(var_row[0:1, 0:512], ps[1][0:1, :]), incs=[(sV, 1)])
        p.op(VEC, lambda e: e.tensor_copy(var_row[0:1, 512:1024], ps[6][0:1, :]), incs=[(sV, 1)])
        p.op(VEC, lambda e: e.tensor_sub(var_row, var_row, scr_row), incs=[(sV, 1)])
        rsqrt_nr(r2_row, var_row, scr_row, var_row)
        rows5 = sV.n
        for hh in range(2):
            p.op(TEN, (lambda h_: lambda e: e.matmul(
                ps[2 + h_][:], onerow_f[:], mu2_row[0:1, h_ * 512:(h_ + 1) * 512],
                start=True, stop=True))(hh),
                 waits=([(sV, rows5)] if hh == 0 else []), incs=[(sT, 1)])
            p.op(TEN, (lambda h_: lambda e: e.matmul(
                ps[4 if h_ == 0 else 7][:], onerow_f[:], r2_row[0:1, h_ * 512:(h_ + 1) * 512],
                start=True, stop=True))(hh), incs=[(sT, 1)])
        reps5_mm = sT.n

        nrm_evts = {}
        ff_cnt = [0, 0]
        for k in range(NDC):
            w = [(sT, reps5_mm)]
            if k >= 2:
                w.append((sV, nrm_evts[k - 2]))
                w.append((sff.sems[k % 2], 16 * ff_cnt[k % 2]))
            else:
                w += [(sV, st_evts[NDC - 2 + k][0]), (sS, st_evts[NDC - 2 + k][1])]
            ld = dma(SCAL, y_in[:, k % 2], y_scr[k], sy5.sems[k % 2], waits=w)
            ld2 = dma(SCAL, xT_st[:, k % 2], x_ownT[k * 128:(k + 1) * 128, :],
                      sxT.sems[k % 2], waits=(list(w) if k < 2 else []))
            for hh in range(2):
                p.op(VEC, (lambda k_, h_: lambda e: e.tensor_sub(
                    t_nrm[:, k_ % 2, h_ * 512:(h_ + 1) * 512],
                    y_in[:, k_ % 2, h_ * 512:(h_ + 1) * 512], ps[2 + h_][:]))(k, hh),
                     waits=([ld] if hh == 0 else []), incs=[(sV, 1)])
                p.op(VEC, (lambda k_, h_: lambda e: e.tensor_mul(
                    t_nrm[:, k_ % 2, h_ * 512:(h_ + 1) * 512],
                    t_nrm[:, k_ % 2, h_ * 512:(h_ + 1) * 512],
                    ps[4 if h_ == 0 else 7][:]))(k, hh), incs=[(sV, 1)])
            vv = sV.n
            w2 = [(sV, vv)]
            if k == 0:
                w2.append((sV, consts_done))
            if ff_cnt[k % 2] > 0:
                w2.append((sff.sems[k % 2], 16 * ff_cnt[k % 2]))
            p.op(VEC, (lambda k_: lambda e: e.tensor_scalar(
                y_in[:, k_ % 2], t_nrm[:, k_ % 2],
                gam_fm[:, k_:k_ + 1], bet_fm[:, k_:k_ + 1], OP.mult, OP.add))(k),
                 waits=w2, incs=[(sV, 1)])
            p.op(VEC, (lambda k_: lambda e: e.tensor_add(
                y_in[:, k_ % 2], y_in[:, k_ % 2], xT_st[:, k_ % 2]))(k),
                 waits=[ld2], incs=[(sV, 1)])
            nrm_evts[k] = sV.n
            dma(SCAL, xffn_scr[k], y_in[:, k % 2], sff.sems[k % 2], waits=[(sV, sV.n)])
            ff_cnt[k % 2] += 1
        xffn_all = sff.totals()
        nrm_last = sV.n

        if stop_after <= 5:
            return finalize()

        last_ten = sT.n
        for k in range(NDC):
            p.op(VEC, (lambda k_: lambda e: e.tensor_scalar_mul(
                xn_fm[:, k_], xn_fm[:, k_], cb_fm[:, k_:k_ + 1]))(k),
                 waits=([(sT, last_ten)] if k == 0 else []), incs=[(sV, 1)])
        qp_evt = sV.n
        qt_prev_copy = 0
        for hd in range(4):
            for half in range(2):
                for j in range(8):
                    w = []
                    if hd == 0 and half == 0 and j == 0:
                        w.append((sV, qp_evt))
                    if j == 0 and qt_prev_copy:
                        w.append((sV, qt_prev_copy))
                    p.op(TEN, (lambda h_, hf_, j_: lambda e: e.matmul(
                        ps[0][0:1, :], ones_bf[:], xn_fm[:, h_ * 8 + j_,
                                                         hf_ * 512:(hf_ + 1) * 512],
                        start=(j_ == 0), stop=(j_ == 7)))(hd, half, j),
                         waits=w, incs=([(sT, 1)] if j == 7 else []))
                tt = sT.n
                p.op(VEC, (lambda h_, hf_: lambda e: e.tensor_copy(
                    qt_sb[0:1, h_ * 1024 + hf_ * 512: h_ * 1024 + (hf_ + 1) * 512],
                    ps[0][0:1, :]))(hd, half),
                     waits=[(sT, tt)] + ([(sV, rw_done)] if (hd == 0 and half == 0) else []),
                     incs=[(sV, 1)])
                qt_prev_copy = sV.n
        qt_done = sV.n
        p.op(GPS, lambda e: e.iota(diff_t, pattern=[[-1, 512]], base=0,
                                   channel_multiplier=1,
                                   allow_small_or_imprecise_dtypes=True),
             waits=[(sV, nrm_last)], incs=[(sG, 1)])
        p.op(GPS, lambda e: e.iota(thr_tab[:, 0:64], pattern=[[-128, 64]], base=0,
                                   channel_multiplier=0,
                                   allow_small_or_imprecise_dtypes=True), incs=[(sG, 1)])
        p.op(GPS, lambda e: e.iota(thr_tab[:, 64:128], pattern=[[-128, 64]], base=512,
                                   channel_multiplier=0,
                                   allow_small_or_imprecise_dtypes=True), incs=[(sG, 1)])
        p.op(GPS, lambda e: e.tensor_scalar_add(thr_tab[:], thr_tab[:], tb_col[:]),
             waits=[(sV, tb_done)], incs=[(sG, 1)])
        thr_done = sG.n

        cache = lambda st: big[:, st * 512:(st + 1) * 512]
        out_cnt = [0, 0, 0, 0]
        n_out = 0
        prev_B_last_mm = big_free_evt
        prev_evac = 0
        kp_cnt = 0
        grp_guard = {}
        gidx = 0
        kb_cnt = [0, 0]
        prev_stt = {}
        prev_sig = {}
        prev_attn = {}
        prev_m_evt = {}

        for sp in range(npass * 4):
            pss, hd = sp // 4, sp % 4
            stt_evt = {}
            sig_evt = {}
            attn_evt = {}
            kp_evt = {}
            nmrow_evt = {}
            scores_mm_evt = {}
            m_evt = {}
            for st in range(NST):
                blk = st // 4
                if st % 4 == 0:
                    ksl = blk % 2
                    w = list(xbf_all) + [negmu_wr]
                    if sp == 0:
                        w += list(moe_all)
                    if blk >= 2:
                        w.append((sT, scores_mm_evt[4 * (blk - 1) - 1]))
                    elif blk == 1:
                        pass
                    if blk <= 1:
                        w.append((sT, prev_B_last_mm))
                        w.append((sV, qt_done))
                        if prev_evac:
                            w.append((sV, prev_evac))
                    nmrow_evt[blk] = dma(SCAL, negmu_row[0:1, ksl],
                                         bounce_bf[4 * blk:4 * blk + 4, :]
                                         .rearrange("a b -> (a b)")[None, :],
                                         snm.sems[ksl], waits=list(w))
                    for j in range(8):
                        dmaT(SYNC, Kp[:, ksl, j],
                             xbf_scr[blk * 512:(blk + 1) * 512,
                                     (hd * 8 + j) * 128:(hd * 8 + j + 1) * 128],
                             skp.sems[ksl], waits=(w if j == 0 else []))
                        kb_cnt[ksl] += 1
                    kp_evt[blk] = (skp.sems[ksl], 16 * kb_cnt[ksl])
                bank = ps[st % 2]
                for j in range(8):
                    w = []
                    if j == 0:
                        w.append(kp_evt[blk])
                        if st >= 2:
                            w.append((sV, stt_evt[st - 2]))
                        elif sp > 0:
                            w.append((sV, prev_stt[NST - 2 + st]))
                    p.op(TEN, (lambda s_, j_, bk, h_, q_: lambda e: e.matmul(
                        bk[:], Kp[:, (s_ // 4) % 2, j_, (s_ % 4) * 128:((s_ % 4) + 1) * 128],
                        xn_fm[:, h_ * 8 + j_, q_ * 512:(q_ + 1) * 512],
                        start=(j_ == 0), stop=False))(st, j, bank, hd, pss),
                         waits=w, incs=[])
                p.op(TEN, (lambda s_, bk, b_, h_, q_: lambda e: e.matmul(
                    bk[:], negmu_row[0:1, b_ % 2, (s_ % 4) * 128:((s_ % 4) + 1) * 128],
                    qt_sb[0:1, h_ * 1024 + q_ * 512: h_ * 1024 + (q_ + 1) * 512],
                    start=False, stop=True))(st, bank, blk, hd, pss),
                     waits=[nmrow_evt[blk]], incs=[(sT, 1)])
                scores_mm_evt[st] = sT.n
                w = [(sG, thr_done)]
                p.op(VEC, (lambda s_, q_: lambda e: e.tensor_scalar(
                    mask01[:, s_ % 2], diff_t,
                    thr_tab[:, q_ * 64 + s_: q_ * 64 + s_ + 1], None, OP.is_le))(st, pss),
                     waits=w, incs=[(sV, 1)])
                w = []
                if st >= 2:
                    w.append((sT, m_evt[st - 2]))
                elif sp > 0:
                    w.append((sT, prev_m_evt[NST - 2 + st]))
                p.op(VEC, (lambda s_: lambda e: e.tensor_scalar(
                    mask05a[:, s_ % 2], mask01[:, s_ % 2], 0.5, None, OP.mult))(st),
                     waits=w, incs=[(sV, 1)])
                w = [(sT, scores_mm_evt[st])]
                if st >= 2:
                    w.append((sS, sig_evt[st - 2]))
                elif sp > 0:
                    w.append((sS, prev_sig[NST - 2 + st]))
                p.op(VEC, (lambda s_, bk: lambda e: e.tensor_scalar(
                    sig_in[:, s_ % 2], bk[:], stat_r8[:, s_:s_ + 1], None, OP.mult))(st, bank),
                     waits=w, incs=[(sV, 1)])
                stt_evt[st] = sV.n
                w = [(sV, stt_evt[st])]
                if st >= 2:
                    w.append((sV, attn_evt[st - 2]))
                elif sp > 0:
                    w.append((sV, prev_attn[NST - 2 + st]))
                p.op(SCAL, (lambda s_: lambda e: e.activation(
                    sg_t[:, s_ % 2], sig_in[:, s_ % 2], ACT.Sigmoid))(st),
                     waits=w, incs=[(sS, 1)])
                sig_evt[st] = sS.n
                w = [(sS, sig_evt[st])]
                if st == 0:
                    w.append((sT, prev_B_last_mm))
                p.op(VEC, (lambda s_: lambda e: e.scalar_tensor_tensor(
                    cache(s_), sg_t[:, s_ % 2], -0.5, mask01[:, s_ % 2],
                    OP.add, OP.mult))(st),
                     waits=w, incs=[(sV, 1)])
                attn_evt[st] = sV.n
                w = [(sV, attn_evt[st])]
                if sp == 0 and st == 0:
                    w.append((sV, negmu_psum_free))
                p.op(TEN, (lambda s_: lambda e: e.matmul(
                    ps[4][0:1, :], stat_negmur_bf[:, s_:s_ + 1], cache(s_),
                    start=(s_ == 0), stop=False))(st),
                     waits=w, incs=[])
                p.op(TEN, (lambda s_: lambda e: e.matmul(
                    ps[4][0:1, :], stat_negmur_bf[:, s_:s_ + 1], mask05a[:, s_ % 2],
                    start=False, stop=(s_ == NST - 1)))(st),
                     waits=[], incs=[(sT, 1)])
                m_evt[st] = sT.n
            m_mm_evt = m_evt[NST - 1]
            p.op(VEC, lambda e: e.tensor_copy(m_row[:], ps[4][0:1, :]),
                 waits=[(sT, m_mm_evt)], incs=[(sV, 1)])
            mrow_evt = sV.n

            rb_evts = {}
            for dsub in range(4):
                chunk = hd * 8 + dsub
                if dsub == 0:
                    w = list(xffn_all)
                    if prev_evac:
                        w.append((sV, prev_evac))
                else:
                    w = []
                rb_evts[dsub] = dma(SCAL, xffn_rb[:, dsub % 4],
                                    xffn_scr[chunk][:, pss * 512:(pss + 1) * 512],
                                    sfr.sems[dsub % 4], waits=w)
            grp_mm = {}
            for st in range(NST):
                w = []
                sl = st % 2
                if gidx >= 2:
                    w.append(grp_guard[gidx - 2])
                else:
                    w.append((sT, big_free_evt))
                vld = dma(SYNC, v_t[:, sl],
                          xbf_scr[st * 128:(st + 1) * 128,
                                  hd * 1024:(hd + 1) * 1024], svl.sems[sl], waits=w)
                p.op(VEC, (lambda s_: lambda e: e.tensor_scalar_mul(
                    v_t[:, s_ % 2], v_t[:, s_ % 2], stat_r[:, s_:s_ + 1]))(st),
                     waits=[vld] + ([grp_guard[gidx - 2]] if gidx >= 2 else []),
                     incs=[(sV, 1)])
                xr_evt = sV.n
                p.op(VEC, (lambda s_, q_: lambda e: e.tensor_scalar(
                    mask05b[:, s_ % 2], diff_t,
                    thr_tab[:, q_ * 64 + s_: q_ * 64 + s_ + 1], 0.5,
                    OP.is_le, OP.mult))(st, pss),
                     waits=([grp_guard[gidx - 2]] if gidx >= 2 else []), incs=[(sV, 1)])
                m5_evt = sV.n
                for dsub in range(8):
                    w2 = []
                    if dsub == 0:
                        w2.append((sV, m5_evt))
                        if st == 0:
                            w2.append((sV, mrow_evt))
                            w2.append((sV, stt_evt[NST - 1]))
                    p.op(TEN, (lambda s_, d_: lambda e: e.matmul(
                        ps[d_][:], v_t[:, s_ % 2, d_ * 128:(d_ + 1) * 128], cache(s_),
                        start=(s_ == 0), stop=False))(st, dsub),
                         waits=w2, incs=[])
                    p.op(TEN, (lambda s_, d_: lambda e: e.matmul(
                        ps[d_][:], v_t[:, s_ % 2, d_ * 128:(d_ + 1) * 128],
                        mask05b[:, s_ % 2],
                        start=False, stop=False))(st, dsub),
                         waits=[], incs=([(sT, 1)] if dsub == 7 else []))
                grp_mm[st] = sT.n
                grp_guard[gidx] = (sT, sT.n)
                gidx += 1
            for dsub in range(8):
                chunk = hd * 8 + dsub
                p.op(TEN, (lambda d_: lambda e: e.matmul(
                    ps[d_][:], onerow_bf[:], m_row[:], start=False, stop=True))(dsub),
                     incs=[(sT, 1)])
                r1 = sT.n
                w = [(sT, r1), rb_evts[dsub]]
                sl = n_out % 4
                if out_cnt[sl] > 0:
                    w.append((sou.sems[sl], 16 * out_cnt[sl]))
                p.op(VEC, (lambda d_, c_, n_: lambda e: e.scalar_tensor_tensor(
                    out_st[:, n_ % 4], ps[d_][:], vb_fm[:, c_:c_ + 1],
                    xffn_rb[:, d_ % 4], OP.mult, OP.add))(dsub, chunk, n_out),
                     waits=w, incs=[(sV, 1)])
                vv = sV.n
                dma(SCAL, out_fm[chunk][:, pss * 512:(pss + 1) * 512],
                    out_st[:, n_out % 4], sou.sems[sl], waits=[(sV, vv)])
                out_cnt[sl] += 1
                n_out += 1
                if dsub + 4 < 8:
                    rb_evts[dsub + 4] = dma(
                        SCAL, xffn_rb[:, (dsub + 4) % 4],
                        xffn_scr[hd * 8 + dsub + 4][:, pss * 512:(pss + 1) * 512],
                        sfr.sems[(dsub + 4) % 4], waits=[(sV, vv)])
            prev_evac = sV.n
            prev_B_last_mm = grp_mm[NST - 1]
            prev_stt, prev_sig, prev_attn = stt_evt, sig_evt, attn_evt
            prev_m_evt = m_evt

        pass
        return finalize()
    return nc


_NC_CACHE = {}


def kernel(**inputs):
    x = np.asarray(inputs["L0_attn_x"], dtype=np.float32).reshape(T_ALL, D)
    if "nc" not in _NC_CACHE:
        _NC_CACHE["nc"] = build_program()
    nc = _NC_CACHE["nc"]

    common = {
        "x_full": np.ascontiguousarray(x),
        "up_w": np.ascontiguousarray(inputs["L0_ffn_up_w"], dtype=np.float32),
        "down_w": np.ascontiguousarray(inputs["L0_ffn_down_w"], dtype=np.float32),
        "fc1_w": np.ascontiguousarray(inputs["L0_ffn_fc1_w"], dtype=np.float32),
        "fc2_w": np.ascontiguousarray(inputs["L0_ffn_fc2_w"], dtype=np.float32),
        "exp_w": np.ascontiguousarray(inputs["L0_ffn_expert_w"], dtype=np.float32),
        "gate_wT": np.ascontiguousarray(np.asarray(inputs["L0_ffn_gate_w"], dtype=np.float32).T),
        "gamma": np.ascontiguousarray(np.asarray(inputs["L0_ffn_ln_gamma"], dtype=np.float32)),
        "beta": np.ascontiguousarray(np.asarray(inputs["L0_ffn_ln_beta"], dtype=np.float32)),
        "bv_q": np.ascontiguousarray(inputs["L0_attn_bv_q"], dtype=np.float32),
        "bv_k": np.ascontiguousarray(inputs["L0_attn_bv_k"], dtype=np.float32),
        "bv_v": np.ascontiguousarray(inputs["L0_attn_bv_v"], dtype=np.float32),
    }
    in_maps = []
    for c in range(N_CORES):
        shard = x[c * T_OWN:(c + 1) * T_OWN]
        m = dict(common)
        m["x_own"] = np.ascontiguousarray(shard)
        m["x_ownT"] = np.ascontiguousarray(shard.T)
        m["tb_in"] = np.array([[c * T_OWN]], dtype=np.float32)
        in_maps.append(m)

    res = bass_utils.run_bass_kernel_spmd(nc, in_maps, core_ids=list(range(N_CORES)))
    out = np.empty((T_ALL, D), dtype=np.float32)
    for c in range(N_CORES):
        o = np.asarray(res.results[c]["out_fm"])
        out[c * T_OWN:(c + 1) * T_OWN] = o.transpose(2, 0, 1).reshape(T_OWN, D)
    return out.reshape(1, T_ALL, D)


# version 04
# speedup vs baseline: 1.3052x; 1.3052x over previous
import numpy as np
from contextlib import ExitStack

import concourse.bass as bass
import concourse.mybir as mybir
import concourse.bass_utils as bass_utils

F32 = mybir.dt.float32
BF16 = mybir.dt.bfloat16
AX = mybir.AxisListType
OP = mybir.AluOpType
ACT = mybir.ActivationFunctionType

T_ALL = 8192
T_OWN = 1024
D = 4096
INT = 8192
LAT = 2048
NE = 8
NDC = D // 128
NST = T_ALL // 128
EPS = 1e-5
MASK_NEG = -30000.0
N_CORES = 8


class Sem:
    def __init__(self, es, nc, name):
        self.h = es.enter_context(nc.semaphore(name))
        self.n = 0


class Prog:
    def __init__(self):
        self.ops = {k: [] for k in ("sync", "scalar", "vector", "tensor", "gpsimd")}
        self.eng_prim = {}
        self.eng_state = {k: {} for k in self.ops}

    def set_engine_sems(self, m):
        self.eng_prim = m

    def op(self, eng, emit, waits=(), incs=(), dma=False):
        waits = [(s, v) for (s, v) in waits if v > 0]
        incs = list(incs)
        if eng in self.eng_prim and emit is not None and not dma:
            for s, c in self.eng_state[eng].items():
                if c > 0:
                    waits.append((s, c))
            if not incs:
                incs.append((self.eng_prim[eng], 1))
        dd = {}
        for s, v in waits:
            if id(s) not in dd or dd[id(s)][1] < v:
                dd[id(s)] = (s, v)
        waits = list(dd.values())
        for s, v in waits:
            assert v <= s.n, f"wait target {v} > scheduled count {s.n} ({eng})"
        inc_rec = []
        for s, k in incs:
            s.n += k
            inc_rec.append((s, k))
            if eng in self.eng_prim and not dma:
                st = self.eng_state[eng]
                st[s] = st.get(s, 0) + k
        self.ops[eng].append((waits, emit, inc_rec))

    def emit_engine(self, name, eng):
        for waits, emit, incs in self.ops[name]:
            for s, v in waits:
                eng.wait_ge(s.h, v)
            instr = emit(eng) if emit is not None else None
            if incs:
                if instr is None:
                    instr = eng.nop()
                for s, k in incs:
                    instr.then_inc(s.h, k)


class Stream:

    def __init__(self, es, nc, name, slots):
        self.sems = [Sem(es, nc, f"{name}{i}") for i in range(slots)]
        self.slots = slots

    def sem(self, k):
        return self.sems[k % self.slots]

    def totals(self):
        return [(s, s.n) for s in self.sems]


_LAST_PROG = None


def build_program(stop_after=99, npass=2):
    nc = bass.Bass("TRN2", target_bir_lowering=False)

    dr = lambda n, s, dt, k="ExternalInput": nc.dram_tensor(n, s, dt, kind=k).ap()
    x_own = dr("x_own", [T_OWN, D], F32)
    x_ownT = dr("x_ownT", [D, T_OWN], F32)
    x_full = dr("x_full", [T_ALL, D], F32)
    tb_in = dr("tb_in", [1, 1], F32)
    up_w = dr("up_w", [D, INT], F32)
    down_w = dr("down_w", [INT, D], F32)
    fc1_w = dr("fc1_w", [D, LAT], F32)
    fc2_w = dr("fc2_w", [LAT, D], F32)
    exp_w = dr("exp_w", [LAT, LAT], F32)
    gate_wT = dr("gate_wT", [NE, D], F32)
    gamma = dr("gamma", [D], F32)
    beta = dr("beta", [D], F32)
    bv_q = dr("bv_q", [4, 1024], F32)
    bv_k = dr("bv_k", [4, 1024], F32)
    bv_v = dr("bv_v", [4, 1024], F32)

    out_fm = dr("out_fm", [NDC, 128, T_OWN], F32, "ExternalOutput")

    xbf_scr = dr("xbf_scr", [T_ALL, D], BF16, "Internal")
    h_scr = dr("h_scr", [INT // 128, 128, T_OWN], BF16, "Internal")
    y_scr = dr("y_scr", [NDC, 128, T_OWN], F32, "Internal")
    xffn_scr = dr("xffn_scr", [NDC, 128, T_OWN], F32, "Internal")
    moe_scr = dr("moe_scr", [16, 128, T_OWN], BF16, "Internal")
    bounce_f = dr("bounce_f", [64, 128], F32, "Internal")
    bounce_bf = dr("bounce_bf", [64, 128], BF16, "Internal")
    cb_scr = dr("cb_scr", [4, 1024], F32, "Internal")
    vb_scr = dr("vb_scr", [4, 1024], F32, "Internal")

    p = Prog()
    global _LAST_PROG
    _LAST_PROG = p
    SYNC, SCAL, VEC, TEN, GPS = "sync", "scalar", "vector", "tensor", "gpsimd"

    with ExitStack() as es:
        sb = lambda n, s, dt: es.enter_context(nc.sbuf_tensor(n, s, dt))

        xn_fm = sb("xn_fm", [128, NDC, T_OWN], BF16)
        big = sb("big", [128, 32768], BF16)
        AF = sb("AF", [128, 8192], F32)
        AB = sb("AB", [128, 10240], BF16)
        stat_mu = sb("stat_mu", [128, NST], F32)
        stat_negmu_bf = sb("stat_negmu_bf", [128, NST], BF16)
        stat_negmur_bf = sb("stat_negmur_bf", [128, NST], BF16)
        stat_r = sb("stat_r", [128, NST], F32)
        stat_r8 = sb("stat_r8", [128, NST], F32)
        own_mu = sb("own_mu", [128, 8], F32)
        own_r = sb("own_r", [128, 8], F32)
        cb_fm = sb("cb_fm", [128, 32], F32)
        vb_fm = sb("vb_fm", [128, 32], F32)
        gam_fm = sb("gam_fm", [128, 32], F32)
        bet_fm = sb("bet_fm", [128, 32], F32)
        gate_fm = sb("gate_fm", [128, 32, NE], BF16)
        rw_rep = sb("rw_rep", [128, T_OWN], BF16)
        negmu_stage = sb("negmu_stage", [64, 128], BF16)
        negmu_row = sb("negmu_row", [1, 2, 512], BF16)
        qt_row = sb("qt_row", [1, T_OWN], BF16)
        qt_sb = sb("qt_sb", [1, D], BF16)
        mask01 = sb("mask01", [128, 2, 512], BF16)
        mask05a = sb("mask05a", [128, 2, 512], BF16)
        mask05b = sb("mask05b", [128, 2, 512], BF16)
        thr_tab = sb("thr_tab", [128, 128], F32)
        m_row = sb("m_row", [1, 512], BF16)
        ones_bf = sb("ones_bf", [128, 1], BF16)
        oneq_bf = sb("oneq_bf", [128, 1], BF16)
        onerow_bf = sb("onerow_bf", [1, 128], BF16)
        onerow_f = sb("onerow_f", [1, 128], F32)
        ident_f = sb("ident_f", [128, 128], F32)
        tb_col = sb("tb_col", [128, 1], F32)
        thr_col = sb("thr_col", [128, 1], F32)
        tiny = sb("tiny", [128, 16], F32)
        pack16 = sb("pack16", [128, 16], F32)
        bv_alpha = sb("bv_alpha", [4, 3], F32)
        cbvb_sb = sb("cbvb_sb", [32, 128], F32)
        stage_t16 = cbvb_sb[0:16, :]
        logit_s = sb("logit_s", [128, NE], F32)

        rs = lambda ap, spec, **kw: ap.rearrange(spec, **kw)
        bvq_sb = AF[0:4, 0:1024]
        bvk_sb = AF[0:4, 1024:2048]
        bvv_sb = AF[0:4, 2048:3072]
        bv_tmp = AF[0:4, 3072:4096]
        x_tm = rs(AF[:, 0:4096], "p (a b) -> p a b", a=2)
        xT_st = rs(AF[:, 4096:6144], "p (a b) -> p a b", a=2)
        ownr_row = AF[0:1, 6144:7168]
        ownm_row = AF[0:1, 7168:8192]
        gate_sb = AF[0:8, 0:4096]
        rw_row = AF[0:1, 7168:8192]
        w_raw = rs(AF[:, 4096:6144], "p (a b) -> p a b", a=4)
        relu_s = AF[:, 6144:7168]
        y_stage = rs(AF[:, 0:2048], "p (a b) -> p a b", a=4)
        y_rb = rs(AF[:, 2048:4096], "p (a b) -> p a b", a=4)
        y_in = rs(AF[:, 0:2048], "p (a b) -> p a b", a=2)
        t_nrm = rs(AF[:, 2048:4096], "p (a b) -> p a b", a=2)
        mu2_row = AF[0:1, 4096:5120]
        r2_row = AF[0:1, 5120:6144]
        var_row = AF[0:1, 6144:7168]
        scr_row = AF[0:1, 7168:8192]
        out_st = rs(AF[:, 0:2048], "p (a b) -> p a b", a=4)
        sig_in = rs(AF[:, 2048:3072], "p (a b) -> p a b", a=2)
        mask_t = rs(AF[:, 3072:4096], "p (a b) -> p a b", a=2)
        sg_t = rs(AF[:, 4096:5120], "p (a b) -> p a b", a=2)
        xffn_rb = rs(AF[:, 5120:7168], "p (a b) -> p a b", a=4)
        w_bf = rs(AB[:, 0:2048], "p (a b) -> p a b", a=4)
        h_stage = rs(AB[:, 2048:4096], "p (a b) -> p a b", a=2)
        moe_stage = rs(AB[:, 2048:4096], "p (a b) -> p a b", a=2)
        xbf_st = rs(AB[:, 4096:8192], "p (a b) -> p a b", a=2)
        ybf = rs(AB[:, 2048:4096], "p (a b) -> p a b", a=2)
        ysq = rs(AB[:, 4096:6144], "p (a b) -> p a b", a=2)
        Kp = rs(AB[:, 0:8192], "p (a c b) -> p a c b", a=2, c=8)
        diff_t = AF[:, 7168:7680]
        v_t = rs(AB[:, 8192:10240], "p (a b) -> p a b", a=2)
        moe_rb = rs(AB[:, 8192:10240], "p (a b) -> p a b", a=2)
        lat_v = rs(big[:, 0:16384], "p (a b) -> p a b", a=16)

        ps = [es.enter_context(nc.psum_tensor(f"ps{i}", [128, 512], F32)) for i in range(8)]

        sm = lambda n: Sem(es, nc, n)
        sV, sS, sT, sWC, sG = sm("sV"), sm("sS"), sm("sT"), sm("sWC"), sm("sG")
        p.set_engine_sems({"vector": sV, "scalar": sS, "gpsimd": sG})
        st_ = lambda n, k: Stream(es, nc, n, k)
        stb = st_("stb", 1)
        sbv = st_("sbv", 1)
        scv = st_("scv", 2)
        sbn = st_("sbn", 1)
        swt = st_("swt", 4)
        sgt = st_("sgt", 1)
        sxt = st_("sxt", 2)
        sxT = st_("sxT", 2)
        sxb = st_("sxb", 2)
        shw = st_("shw", 2)
        smw = st_("smw", 2)
        shl = st_("shl", 1)
        sml = st_("sml", 2)
        skp = st_("skp", 2)
        svl = st_("svl", 2)
        syw = st_("syw", 4)
        syr = st_("syr", 4)
        sy5 = st_("sy5", 2)
        sff = st_("sff", 2)
        sfr = st_("sfr", 4)
        snm = st_("snm", 2)
        sou = st_("sou", 4)
        all_dma_streams = [stb, sbv, scv, sbn, swt, sgt, sxt, sxT, sxb, shw, smw, shl,
                           sml, skp, svl, syw, syr, sy5, sff, sfr, snm, sou]

        def finalize():
            waits = [(sV, sV.n), (sS, sS.n), (sT, sT.n), (sWC, sWC.n), (sG, sG.n)]
            for stq in all_dma_streams:
                waits += stq.totals()
            p.op(SYNC, None, waits=waits)
            with nc.Block() as block:
                block.sync(lambda e: p.emit_engine("sync", e))
                block.scalar(lambda e: p.emit_engine("scalar", e))
                block.vector(lambda e: p.emit_engine("vector", e))
                block.tensor(lambda e: p.emit_engine("tensor", e))
                block.gpsimd(lambda e: p.emit_engine("gpsimd", e))
            return nc

        def dma(q, out, in_, sem, waits=()):
            p.op(q, (lambda o, i: (lambda e: e.dma_start(out=o, in_=i)))(out, in_),
                 waits=waits, incs=[(sem, 16)], dma=True)
            return (sem, sem.n)

        def dmaT(q, out, in_, sem, waits=()):
            p.op(q, (lambda o, i: (lambda e: e.dma_start_transpose(out=o, in_=i)))(out, in_),
                 waits=waits, incs=[(sem, 16)], dma=True)
            return (sem, sem.n)

        p.op(GPS, lambda e: e.iota(ident_f[:], pattern=[[-1, 128]], base=0,
                                   channel_multiplier=1,
                                   allow_small_or_imprecise_dtypes=True), incs=[(sG, 1)])
        gi = sG.n
        p.op(VEC, lambda e: e.tensor_scalar(ident_f[:], ident_f[:], 0.0, None, OP.is_equal),
             waits=[(sG, gi)], incs=[(sV, 1)])
        ident_done = sV.n
        p.op(VEC, lambda e: e.memset(ones_bf[:], 1.0), incs=[(sV, 1)])
        p.op(VEC, lambda e: e.memset(oneq_bf[:], 1.0 / 4096.0), incs=[(sV, 1)])
        p.op(VEC, lambda e: e.memset(onerow_bf[:], 1.0), incs=[(sV, 1)])
        p.op(VEC, lambda e: e.memset(onerow_f[:], 1.0), incs=[(sV, 1)])
        consts_memset = sV.n

        tb_ev = dma(SCAL, tiny[0:1, 0:1], tb_in[:], stb.sem(0))
        p.op(TEN, lambda e: e.matmul(ps[3][0:128, 0:1], onerow_f[:], tiny[0:1, 0:1],
                                     start=True, stop=True),
             waits=[tb_ev, (sV, consts_memset)], incs=[(sT, 1)])
        p.op(VEC, lambda e: e.tensor_copy(tb_col[:], ps[3][0:128, 0:1]),
             waits=[(sT, sT.n)], incs=[(sV, 1)])
        tb_done = sV.n

        dma(SCAL, bvq_sb, bv_q[:], sbv.sem(0))
        dma(SCAL, bvk_sb, bv_k[:], sbv.sem(0))
        bv_ev = dma(SCAL, bvv_sb, bv_v[:], sbv.sem(0))

        def bind(src, col):
            a = bv_alpha[:, col:col + 1]
            p.op(VEC, (lambda s_, a_: lambda e: e.reduce_sum(
                a_, s_, axis=AX.X, apply_absolute_value=True))(src, a),
                 waits=[bv_ev], incs=[(sV, 1)])
            p.op(VEC, (lambda a_: lambda e: e.tensor_scalar_mul(a_, a_, 1.0 / 1024.0))(a),
                 incs=[(sV, 1)])
            vv = sV.n
            p.op(SCAL, (lambda s_: lambda e: e.sign(bv_tmp, s_))(src),
                 waits=[(sV, vv)], incs=[(sS, 1)])
            ss = sS.n
            p.op(VEC, (lambda a_: lambda e: e.tensor_scalar_mul(bv_tmp, bv_tmp, a_))(a),
                 waits=[(sS, ss)], incs=[(sV, 1)])
            p.op(VEC, (lambda s_: lambda e: e.tensor_scalar(s_, bv_tmp, 0.0, None,
                                                            OP.is_equal))(src), incs=[(sV, 1)])
            p.op(VEC, (lambda s_, a_: lambda e: e.scalar_tensor_tensor(
                s_, s_, a_, bv_tmp, OP.mult, OP.add))(src, a), incs=[(sV, 1)])

        bind(bvq_sb, 0)
        bind(bvk_sb, 1)
        bind(bvv_sb, 2)
        p.op(VEC, lambda e: e.tensor_mul(bvq_sb, bvq_sb, bvk_sb), incs=[(sV, 1)])
        p.op(VEC, lambda e: e.tensor_scalar_mul(bvq_sb, bvq_sb, 0.125), incs=[(sV, 1)])
        binds_done = sV.n

        wcb = dma(SCAL, cb_scr, bvq_sb, scv.sems[0], waits=[(sV, binds_done)])
        wvb = dma(SCAL, vb_scr, bvv_sb, scv.sems[1])

        def to_fm(dst, src_ap, slot, extra, prev_copy):
            ld = dma(SCAL, cbvb_sb[:], src_ap, scv.sems[slot],
                     waits=list(extra) + ([(sV, prev_copy)] if prev_copy else []))
            p.op(TEN, lambda e: e.transpose(ps[0][:, 0:32], cbvb_sb[:], ident_f[0:32, 0:32]),
                 waits=[ld, (sV, ident_done)], incs=[(sT, 1)])
            tt = sT.n
            p.op(VEC, (lambda d: lambda e: e.tensor_copy(d[:], ps[0][:, 0:32]))(dst),
                 waits=[(sT, tt)], incs=[(sV, 1)])
            return sV.n

        e1 = to_fm(cb_fm, cb_scr.rearrange("h (c p) -> (h c) p", p=128), 0, [wcb], 0)
        e2 = to_fm(vb_fm, vb_scr.rearrange("h (c p) -> (h c) p", p=128), 1, [wvb], e1)
        e3 = to_fm(gam_fm, gamma.rearrange("(c p) -> c p", p=128), 0, [], e2)
        consts_done = to_fm(bet_fm, beta.rearrange("(c p) -> c p", p=128), 1, [], e3)

        def rsqrt_nr(dst, var, t1, t2):
            p.op(VEC, (lambda a, b: lambda e: e.tensor_scalar_add(a, b, EPS))(t1, var),
                 incs=[(sV, 1)])
            vv = sV.n
            p.op(SCAL, (lambda a, b: lambda e: e.sqrt(a, b))(t2, t1),
                 waits=[(sV, vv)], incs=[(sS, 1)])
            ss = sS.n
            p.op(VEC, (lambda a, b: lambda e: e.reciprocal(a, b))(dst, t2),
                 waits=[(sS, ss)], incs=[(sV, 1)])
            p.op(VEC, (lambda a, b: lambda e: e.tensor_mul(a, b, b))(t2, dst), incs=[(sV, 1)])
            p.op(VEC, (lambda a, b: lambda e: e.tensor_mul(a, a, b))(t2, t1), incs=[(sV, 1)])
            p.op(VEC, (lambda a: lambda e: e.tensor_scalar(a, a, -0.5, 1.5, OP.mult, OP.add))(t2),
                 incs=[(sV, 1)])
            p.op(VEC, (lambda a, b: lambda e: e.tensor_mul(a, a, b))(dst, t2), incs=[(sV, 1)])

        xbf_cnt = [0, 0]

        def stats_tile(idx, src_hbm, mu_dst, r_dst, c, do_cast):
            for hh in range(2):
                buf = x_tm[:, hh]
                ld = dma(SCAL, buf, src_hbm[idx * 128:(idx + 1) * 128,
                                            hh * 2048:(hh + 1) * 2048], sxt.sems[hh])
                p.op(VEC, (lambda b, col: lambda e: e.reduce_sum(col, b, axis=AX.X))(
                    buf, tiny[:, c + hh:c + hh + 1]), waits=[ld], incs=[(sV, 1)])
                if do_cast:
                    g = []
                    if xbf_cnt[hh] > 0:
                        g.append((sxb.sems[hh], 16 * xbf_cnt[hh]))
                    p.op(VEC, (lambda b, hh_: lambda e: e.tensor_copy(
                        xbf_st[:, hh_], b))(buf, hh), waits=g, incs=[(sV, 1)])
                cv = sV.n
                p.op(SCAL, (lambda b, col: lambda e: e.activation(
                    b, b, ACT.Square, accum_out=col))(buf, tiny[:, c + 2 + hh:c + 3 + hh]),
                     waits=[(sV, cv)], incs=[(sS, 1)])
            sq_s = sS.n
            p.op(VEC, (lambda m: lambda e: e.tensor_add(m, tiny[:, c:c + 1],
                                                        tiny[:, c + 1:c + 2]))(mu_dst),
                 incs=[(sV, 1)])
            p.op(VEC, (lambda m: lambda e: e.tensor_scalar_mul(m, m, 1.0 / D))(mu_dst),
                 incs=[(sV, 1)])
            p.op(VEC, lambda e: e.tensor_add(tiny[:, c + 2:c + 3], tiny[:, c + 2:c + 3],
                                             tiny[:, c + 3:c + 4]),
                 waits=[(sS, sq_s)], incs=[(sV, 1)])
            p.op(VEC, lambda e: e.tensor_scalar_mul(tiny[:, c + 2:c + 3],
                                                    tiny[:, c + 2:c + 3], 1.0 / D),
                 incs=[(sV, 1)])
            p.op(VEC, (lambda m: lambda e: e.tensor_mul(tiny[:, c + 4:c + 5], m, m))(mu_dst),
                 incs=[(sV, 1)])
            p.op(VEC, lambda e: e.tensor_sub(tiny[:, c + 4:c + 5], tiny[:, c + 2:c + 3],
                                             tiny[:, c + 4:c + 5]), incs=[(sV, 1)])
            rsqrt_nr(r_dst, tiny[:, c + 4:c + 5], tiny[:, c + 5:c + 6], tiny[:, c + 2:c + 3])
            if do_cast:
                for hh in range(2):
                    p.op(VEC, (lambda hh_, m_, r_: lambda e: e.tensor_scalar(
                        xbf_st[:, hh_], xbf_st[:, hh_], m_, r_,
                        OP.subtract, OP.mult))(hh, mu_dst, r_dst), incs=[(sV, 1)])
                    dma(SCAL, xbf_scr[idx * 128:(idx + 1) * 128, hh * 2048:(hh + 1) * 2048],
                        xbf_st[:, hh], sxb.sems[hh], waits=[(sV, sV.n)])
                    xbf_cnt[hh] += 1
            return sV.n, sS.n

        own_evt = []
        for i in range(8):
            if i == 0:
                p.op(SCAL, None, waits=[wcb, wvb, (sV, e2)])
            else:
                p.op(SCAL, None, waits=[(sV, own_evt[i - 1][0])])
            own_evt.append(stats_tile(i, x_own, own_mu[:, i:i + 1], own_r[:, i:i + 1],
                                      0, False))

        p.op(VEC, lambda e: e.tensor_copy(pack16[:, 0:8], own_r[:]), incs=[(sV, 1)])
        p.op(VEC, lambda e: e.tensor_mul(pack16[:, 8:16], own_mu[:], own_r[:]), incs=[(sV, 1)])
        p.op(VEC, lambda e: e.tensor_scalar_mul(pack16[:, 8:16], pack16[:, 8:16], -1.0),
             incs=[(sV, 1)])
        pk = sV.n
        p.op(TEN, lambda e: e.transpose(ps[1][0:16, 0:128], pack16[:], ident_f[:]),
             waits=[(sV, pk)], incs=[(sT, 1)])
        tt = sT.n
        p.op(VEC, lambda e: e.tensor_copy(stage_t16, ps[1][0:16, 0:128]),
             waits=[(sT, tt), (sV, consts_done)], incs=[(sV, 1)])
        st16 = sV.n
        bw = dma(SCAL, bounce_f[0:16, :], stage_t16, sbn.sems[0], waits=[(sV, st16)])
        br1 = dma(SCAL, ownr_row, bounce_f[0:8, :].rearrange("a b -> (a b)")[None, :],
                  sbn.sems[0], waits=[bw])
        br2 = dma(SCAL, ownm_row, bounce_f[8:16, :].rearrange("a b -> (a b)")[None, :],
                  sbn.sems[0], waits=[br1])

        for hh in range(2):
            p.op(TEN, (lambda h_: lambda e: e.matmul(
                ps[4 + h_][:], onerow_f[:], ownr_row[0:1, h_ * 512:(h_ + 1) * 512],
                start=True, stop=True))(hh),
                 waits=[br2] if hh == 0 else [], incs=[(sT, 1)])
            p.op(TEN, (lambda h_: lambda e: e.matmul(
                ps[6 + h_][:], onerow_f[:], ownm_row[0:1, h_ * 512:(h_ + 1) * 512],
                start=True, stop=True))(hh), incs=[(sT, 1)])
        reps0_mm = sT.n

        xn_evt = []
        for k in range(NDC):
            buf = xT_st[:, k % 2]
            w = [(sV, xn_evt[k - 2])] if k >= 2 else [(sV, own_evt[7][0])]
            ld = dma(SCAL, buf, x_ownT[k * 128:(k + 1) * 128, :], sxT.sems[k % 2], waits=w)
            for hh in range(2):
                p.op(VEC, (lambda b, h_: lambda e: e.tensor_mul(
                    b[:, h_ * 512:(h_ + 1) * 512], b[:, h_ * 512:(h_ + 1) * 512],
                    ps[4 + h_][:]))(buf, hh),
                     waits=[ld, (sT, reps0_mm)], incs=[(sV, 1)])
                p.op(VEC, (lambda kk, b, h_: lambda e: e.tensor_add(
                    xn_fm[:, kk, h_ * 512:(h_ + 1) * 512],
                    b[:, h_ * 512:(h_ + 1) * 512], ps[6 + h_][:]))(k, buf, hh),
                     incs=[(sV, 1)])
            xn_evt.append(sV.n)
        xn_ready = sV.n

        if stop_after <= 0:
            return finalize()

        gate_ev = dma(SYNC, gate_sb, gate_wT[:], sgt.sem(0),
                      waits=[(sV, own_evt[7][0]), (sS, own_evt[7][1])])
        gfm_evt = 0
        for k in range(NDC):
            w = [gate_ev, (sV, ident_done)]
            if gfm_evt:
                w.append((sV, gfm_evt))
            p.op(TEN, (lambda kk: lambda e: e.transpose(
                ps[2][0:128, 0:8], gate_sb[0:8, kk * 128:(kk + 1) * 128],
                ident_f[0:8, 0:8]))(k),
                 waits=w, incs=[(sT, 1)])
            tt = sT.n
            p.op(VEC, (lambda kk: lambda e: e.tensor_copy(gate_fm[:, kk], ps[2][0:128, 0:8]))(k),
                 waits=[(sT, tt)], incs=[(sV, 1)])
            gfm_evt = sV.n

        l_m1, l_m2, l_cnt, l_two = (tiny[:, 8:9], tiny[:, 9:10], tiny[:, 10:11], tiny[:, 11:12])
        top2_evt = 0
        for tc in range(8):
            for dc in range(NDC):
                w = []
                if tc == 0 and dc == 0:
                    w = [(sV, xn_ready), (sV, gfm_evt), (sV, tb_done)]
                if dc == 0 and tc > 0:
                    w.append((sV, top2_evt))
                p.op(TEN, (lambda t_, d_: lambda e: e.matmul(
                    ps[3][0:128, 0:8], xn_fm[:, d_, t_ * 128:(t_ + 1) * 128], gate_fm[:, d_],
                    start=(d_ == 0), stop=(d_ == NDC - 1)))(tc, dc),
                     waits=w, incs=([(sT, 1)] if dc == NDC - 1 else []))
            mm = sT.n
            lgp = ps[3][0:128, 0:8]
            p.op(VEC, lambda e: e.reduce_max(l_m1, lgp, axis=AX.X), waits=[(sT, mm)],
                 incs=[(sV, 1)])
            p.op(VEC, lambda e: e.tensor_scalar(logit_s[:], lgp, l_m1, None, OP.is_ge),
                 incs=[(sV, 1)])
            p.op(VEC, lambda e: e.reduce_sum(l_cnt, logit_s[:], axis=AX.X), incs=[(sV, 1)])
            p.op(VEC, lambda e: e.scalar_tensor_tensor(logit_s[:], logit_s[:], -1e30, lgp,
                                                       OP.mult, OP.add), incs=[(sV, 1)])
            p.op(VEC, lambda e: e.reduce_max(l_m2, logit_s[:], axis=AX.X), incs=[(sV, 1)])
            p.op(VEC, lambda e: e.tensor_scalar(l_two, l_cnt, 1.5, None, OP.is_ge), incs=[(sV, 1)])
            p.op(VEC, lambda e: e.tensor_sub(l_cnt, l_m1, l_m2), incs=[(sV, 1)])
            p.op(VEC, lambda e: e.tensor_mul(l_cnt, l_cnt, l_two), incs=[(sV, 1)])
            p.op(VEC, lambda e: e.tensor_add(l_cnt, l_cnt, l_m1), incs=[(sV, 1)])
            p.op(VEC, (lambda t_: lambda e: e.tensor_add(pack16[:, t_:t_ + 1], l_cnt, l_m2))(tc),
                 incs=[(sV, 1)])
            top2_evt = sV.n

        p.op(TEN, lambda e: e.transpose(ps[2][0:8, 0:128], pack16[:, 0:8], ident_f[:]),
             waits=[(sV, top2_evt)], incs=[(sT, 1)])
        tt = sT.n
        p.op(VEC, lambda e: e.tensor_copy(stage_t16[0:8, :], ps[2][0:8, 0:128]),
             waits=[(sT, tt)], incs=[(sV, 1)])
        rwv = sV.n
        bw = dma(SCAL, bounce_f[0:8, :], stage_t16[0:8, :], sbn.sems[0],
                 waits=[(sV, rwv), br2])
        br = dma(SCAL, rw_row, bounce_f[0:8, :].rearrange("a b -> (a b)")[None, :],
                 sbn.sems[0], waits=[bw, (sT, reps0_mm)])
        p.op(VEC, lambda e: e.tensor_copy(qt_row[:], rw_row), waits=[br], incs=[(sV, 1)])
        qtc = sV.n
        for hh in range(2):
            p.op(TEN, (lambda h_: lambda e: e.matmul(
                ps[2 + h_][:], onerow_bf[:], qt_row[0:1, h_ * 512:(h_ + 1) * 512],
                start=True, stop=True))(hh),
                 waits=[(sV, qtc)] if hh == 0 else [], incs=[(sT, 1)])
            p.op(VEC, (lambda h_: lambda e: e.tensor_copy(
                rw_rep[:, h_ * 512:(h_ + 1) * 512], ps[2 + h_][:]))(hh),
                 waits=[(sT, sT.n)], incs=[(sV, 1)])
        rw_done = sV.n

        wcast_evt = {}
        mm_evt_of_tile = {}
        wi_box = {"i": 0}

        def wtile(src_ap, extra_dma_waits=()):
            i = wi_box["i"]
            wi_box["i"] = i + 1
            wd = list(extra_dma_waits)
            if i >= 4:
                wd.append((sWC, wcast_evt[i - 4]))
            ld = dma(SYNC, w_raw[:, i % 4], src_ap, swt.sems[i % 4], waits=wd)
            wc = [ld]
            if i >= 4:
                wc.append((sT, mm_evt_of_tile[i - 4]))
            p.op(SCAL, (lambda b_: lambda e: e.copy(w_bf[:, b_], w_raw[:, b_]))(i % 4),
                 waits=wc, incs=[(sWC, 1)])
            wcast_evt[i] = sWC.n
            return i, sWC.n


        fc1_evac_evt = 0
        for lgr in range(4):
            for dc in range(NDC):
                ew = [(sV, xn_ready), (sV, rw_done)] if (lgr == 0 and dc == 0) else []
                i, ce = wtile(fc1_w[dc * 128:(dc + 1) * 128, lgr * 512:(lgr + 1) * 512], ew)
                for sub in range(4):
                    for hf in range(2):
                        w = []
                        if sub == 0 and hf == 0:
                            w.append((sWC, ce))
                            if dc == 0:
                                w.append((sV, xn_ready))
                                w.append((sV, rw_done))
                                if lgr > 0:
                                    w.append((sV, fc1_evac_evt))
                        p.op(TEN, (lambda b_, s_, h_, d_: lambda e: e.matmul(
                            ps[s_ * 2 + h_][:], w_bf[:, b_, s_ * 128:(s_ + 1) * 128],
                            xn_fm[:, d_, h_ * 512:(h_ + 1) * 512],
                            start=(d_ == 0), stop=(d_ == NDC - 1)))(i % 4, sub, hf, dc),
                             waits=w,
                             incs=([(sT, 1)] if (sub == 3 and hf == 1) else []))
                mm_evt_of_tile[i] = sT.n
            mm = sT.n
            for sub in range(4):
                for hf in range(2):
                    p.op(VEC, (lambda l_, s_, h_: lambda e: e.tensor_copy(
                        lat_v[:, l_ * 4 + s_, h_ * 512:(h_ + 1) * 512],
                        ps[s_ * 2 + h_][:]))(lgr, sub, hf),
                         waits=[(sT, mm)], incs=[(sV, 1)])
            fc1_evac_evt = sV.n

        if stop_after <= 1:
            return finalize()

        pro = {"i": 0, "evts": {}}

        def prologue_tiles(n):
            for _ in range(n):
                i = pro["i"]
                if i >= NST:
                    return
                pro["i"] = i + 1
                if i == 0:
                    p.op(SCAL, None, waits=[(sV, gfm_evt), (sV, own_evt[7][0])])
                else:
                    p.op(SCAL, None, waits=[(sV, pro["evts"][i - 1][0])])
                ev = stats_tile(i, x_full, stat_mu[:, i:i + 1], stat_r[:, i:i + 1], 0, True)
                pro["evts"][i] = (ev[0], ev[1])

        NFG = INT // 512
        up_relu_evt = 0
        sq_evts = {}
        hw_cnt = [0, 0]
        for fg in range(NFG):
            for dc in range(NDC):
                ew = [(sV, xn_ready)] if (fg == 0 and dc == 0) else []
                i, ce = wtile(up_w[dc * 128:(dc + 1) * 128, fg * 512:(fg + 1) * 512], ew)
                for sub in range(4):
                    for hf in range(2):
                        w = []
                        if sub == 0 and hf == 0:
                            w.append((sWC, ce))
                            if dc == 0:
                                if fg == 0:
                                    w.append((sV, fc1_evac_evt))
                                else:
                                    w.append((sS, up_relu_evt))
                        p.op(TEN, (lambda b_, s_, h_, d_: lambda e: e.matmul(
                            ps[s_ * 2 + h_][:], w_bf[:, b_, s_ * 128:(s_ + 1) * 128],
                            xn_fm[:, d_, h_ * 512:(h_ + 1) * 512],
                            start=(d_ == 0), stop=(d_ == NDC - 1)))(i % 4, sub, hf, dc),
                             waits=w,
                             incs=([(sT, 1)] if (sub == 3 and hf == 1) else []))
                mm_evt_of_tile[i] = sT.n
            mm = sT.n
            for sub in range(4):
                j = fg * 4 + sub
                for hf in range(2):
                    w = [(sT, mm)]
                    if hf == 0 and j >= 1:
                        w.append((sV, sq_evts[j - 1]))
                    p.op(SCAL, (lambda s_, h_: lambda e: e.activation(
                        relu_s[:, h_ * 512:(h_ + 1) * 512],
                        ps[s_ * 2 + h_][:], ACT.Relu))(sub, hf),
                         waits=w, incs=[(sS, 1)])
                rl = sS.n
                w = [(sS, rl)]
                if hw_cnt[j % 2] > 0:
                    w.append((shw.sems[j % 2], 16 * hw_cnt[j % 2]))
                p.op(VEC, (lambda j_: lambda e: e.tensor_mul(
                    h_stage[:, j_ % 2], relu_s[:], relu_s[:]))(j),
                     waits=w, incs=[(sV, 1)])
                sq_evts[j] = sV.n
                dma(SCAL, h_scr[j], h_stage[:, j % 2], shw.sems[j % 2],
                    waits=[(sV, sq_evts[j])])
                hw_cnt[j % 2] += 1
            up_relu_evt = sS.n
            prologue_tiles(4)
        prologue_tiles(NST)
        stats_done_v = sV.n
        up_done_s = sS.n
        h_all = shw.totals()
        xbf_all = sxb.totals()

        if stop_after <= 2:
            return finalize()

        exp_evac_evt = 0
        smo_guard = {}
        mw_cnt = [0, 0]
        for mg in range(4):
            for lc in range(16):
                i, ce = wtile(exp_w[lc * 128:(lc + 1) * 128, mg * 512:(mg + 1) * 512])
                for sub in range(4):
                    for hf in range(2):
                        w = []
                        if sub == 0 and hf == 0:
                            w.append((sWC, ce))
                            if lc == 0:
                                if mg == 0:
                                    w.append((sS, up_done_s))
                                    w.append((sV, stats_done_v))
                                else:
                                    w.append((sV, exp_evac_evt))
                        p.op(TEN, (lambda b_, s_, h_, l_: lambda e: e.matmul(
                            ps[s_ * 2 + h_][:], w_bf[:, b_, s_ * 128:(s_ + 1) * 128],
                            lat_v[:, l_, h_ * 512:(h_ + 1) * 512],
                            start=(l_ == 0), stop=(l_ == 15)))(i % 4, sub, hf, lc),
                             waits=w,
                             incs=([(sT, 1)] if (sub == 3 and hf == 1) else []))
                mm_evt_of_tile[i] = sT.n
            mm = sT.n
            for sub in range(4):
                m_chunk = mg * 4 + sub
                for hf in range(2):
                    w = [(sT, mm)]
                    if mg == 0 and sub == 0 and hf == 0:
                        w.append((sV, rw_done))
                        w += h_all
                    if mw_cnt[m_chunk % 2] > 0 and hf == 0:
                        w.append((smw.sems[m_chunk % 2], 16 * mw_cnt[m_chunk % 2]))
                    p.op(VEC, (lambda m_, s_, h_: lambda e: e.tensor_mul(
                        moe_stage[:, m_ % 2, h_ * 512:(h_ + 1) * 512], ps[s_ * 2 + h_][:],
                        rw_rep[:, h_ * 512:(h_ + 1) * 512]))(m_chunk, sub, hf),
                         waits=w, incs=[(sV, 1)])
                smo_guard[m_chunk] = dma(SCAL, moe_scr[m_chunk], moe_stage[:, m_chunk % 2],
                                         smw.sems[m_chunk % 2], waits=[(sV, sV.n)])
                mw_cnt[m_chunk % 2] += 1
            exp_evac_evt = sV.n
        moe_all = smw.totals()
        exp_last_mm = sT.n

        if stop_after <= 3:
            return finalize()

        y_evac_evt = exp_evac_evt
        big_last_mm = 0
        hl_cnt = 0
        ywr_cnt = [0, 0, 0, 0]
        yadd1 = []
        yrd_cnt = [0, 0, 0, 0]
        ml_cnt = [0, 0]
        fc2_mm = {}
        y_half0_all = []
        for half in range(2):
            w = list(moe_all) + list(h_all) + [(sT, exp_last_mm)]
            if half == 1:
                w = [(sT, big_last_mm)]
            for fl in range(32):
                f_chunk = half * 32 + fl
                dma(SYNC, big[:, fl * 1024:(fl + 1) * 1024], h_scr[f_chunk], shl.sems[0],
                    waits=(w if fl == 0 else []))
                hl_cnt += 1
            hload_ev = (shl.sems[0], 16 * hl_cnt)
            for g in range(8):
                for fl in range(32):
                    f_chunk = half * 32 + fl
                    i, ce = wtile(down_w[f_chunk * 128:(f_chunk + 1) * 128,
                                         g * 512:(g + 1) * 512])
                    for sub in range(4):
                        for hf in range(2):
                            w2 = []
                            if sub == 0 and hf == 0:
                                w2.append((sWC, ce))
                                if fl == 0:
                                    w2.append(hload_ev)
                                    w2.append((sV, y_evac_evt))
                            p.op(TEN, (lambda b_, s_, h_, fl_, last_: lambda e: e.matmul(
                                ps[s_ * 2 + h_][:], w_bf[:, b_, s_ * 128:(s_ + 1) * 128],
                                big[:, fl_ * 1024 + h_ * 512: fl_ * 1024 + (h_ + 1) * 512],
                                start=(fl_ == 0), stop=last_))(i % 4, sub, hf, fl,
                                                               half == 0 and fl == 31),
                                 waits=w2,
                                 incs=([(sT, 1)] if (sub == 3 and hf == 1) else []))
                    mm_evt_of_tile[i] = sT.n
                if half == 1:
                    for mc in range(16):
                        nm = g * 16 + mc
                        wv = list(moe_all) if nm == 0 else []
                        if nm >= 2:
                            wv.append((sT, fc2_mm[nm - 2]))
                        mld = dma(SYNC, moe_rb[:, nm % 2], moe_scr[mc],
                                  sml.sems[nm % 2], waits=wv)
                        ml_cnt[nm % 2] += 1
                        i, ce = wtile(fc2_w[mc * 128:(mc + 1) * 128, g * 512:(g + 1) * 512])
                        for sub in range(4):
                            for hf in range(2):
                                w2 = []
                                if sub == 0 and hf == 0:
                                    w2 = [(sWC, ce), mld]
                                p.op(TEN, (lambda b_, s_, h_, n_, last_: lambda e: e.matmul(
                                    ps[s_ * 2 + h_][:], w_bf[:, b_, s_ * 128:(s_ + 1) * 128],
                                    moe_rb[:, n_ % 2, h_ * 512:(h_ + 1) * 512],
                                    start=False, stop=last_))(i % 4, sub, hf, nm, mc == 15),
                                     waits=w2,
                                     incs=([(sT, 1)] if (sub == 3 and hf == 1) else []))
                        mm_evt_of_tile[i] = sT.n
                        fc2_mm[nm] = mm_evt_of_tile[i]
                mm = sT.n
                for sub in range(4):
                    dout_chunk = g * 4 + sub
                    for hf in range(2):
                        bank = sub * 2 + hf
                        if half == 0:
                            n = sum(ywr_cnt)
                            sl = n % 4
                            w2 = [(sT, mm)]
                            if ywr_cnt[sl] > 0:
                                w2.append((syw.sems[sl], 16 * ywr_cnt[sl]))
                            p.op(VEC, (lambda bk, n_: lambda e: e.tensor_copy(
                                y_stage[:, n_ % 4], ps[bk][:]))(bank, n),
                                 waits=w2, incs=[(sV, 1)])
                            dma(SCAL, y_scr[dout_chunk][:, hf * 512:(hf + 1) * 512],
                                y_stage[:, n % 4], syw.sems[sl], waits=[(sV, sV.n)])
                            ywr_cnt[sl] += 1
                        else:
                            n = len(yadd1)
                            sl = n % 4
                            w2 = list(y_half0_all)
                            if n >= 4:
                                w2.append((sV, yadd1[n - 4]))
                            ld = dma(SCAL, y_rb[:, n % 4],
                                     y_scr[dout_chunk][:, hf * 512:(hf + 1) * 512],
                                     syr.sems[sl], waits=w2)
                            yrd_cnt[sl] += 1
                            w3 = [(sT, mm), ld]
                            if ywr_cnt[sl] > 0:
                                w3.append((syw.sems[sl], 16 * ywr_cnt[sl]))
                            p.op(VEC, (lambda bk, n_: lambda e: e.tensor_add(
                                y_stage[:, n_ % 4], ps[bk][:], y_rb[:, n_ % 4]))(bank, n),
                                 waits=w3, incs=[(sV, 1)])
                            yadd1.append(sV.n)
                            dma(SCAL, y_scr[dout_chunk][:, hf * 512:(hf + 1) * 512],
                                y_stage[:, n % 4], syw.sems[sl], waits=[(sV, sV.n)])
                            ywr_cnt[sl] += 1
                y_evac_evt = sV.n
            if half == 0:
                big_last_mm = sT.n
                y_half0_all = syw.totals()
        y_all = syw.totals()
        big_free_evt = sT.n
        last_wcast = sWC.n

        if stop_after <= 4:
            return finalize()

        st_evts = {}
        for k in range(NDC):
            w = list(y_all)
            if k >= 2:
                w += [(sV, st_evts[k - 2][0]), (sS, st_evts[k - 2][1])]
            ld = dma(SCAL, y_in[:, k % 2], y_scr[k], sy5.sems[k % 2], waits=w)
            w_ybf = [ld]
            if k == 0:
                w_ybf += [(sV, y_evac_evt)] + list(moe_all) + list(h_all)
            if k >= 2:
                w_ybf.append((sT, st_evts[k - 2][2]))
            p.op(VEC, (lambda k_: lambda e: e.tensor_copy(ybf[:, k_ % 2], y_in[:, k_ % 2]))(k),
                 waits=w_ybf, incs=[(sV, 1)])
            cb_ = sV.n
            w_ysq = [ld]
            if k == 0:
                w_ysq += [(sWC, last_wcast)] + list(xbf_all)
            if k >= 2:
                w_ysq.append((sT, st_evts[k - 2][2]))
            p.op(SCAL, (lambda k_: lambda e: e.activation(ysq[:, k_ % 2], y_in[:, k_ % 2],
                                                          ACT.Square))(k),
                 waits=w_ysq, incs=[(sS, 1)])
            sq_ = sS.n
            for hf in range(2):
                pm = ps[0] if hf == 0 else ps[5]
                p.op(TEN, (lambda k_, h_, pm_: lambda e: e.matmul(
                    pm_[0:1, :], oneq_bf[:], ybf[:, k_ % 2, h_ * 512:(h_ + 1) * 512],
                    start=(k_ == 0), stop=(k_ == NDC - 1)))(k, hf, pm),
                     waits=([(sV, cb_)] if hf == 0 else []), incs=[])
            for hf in range(2):
                pq = ps[1] if hf == 0 else ps[6]
                p.op(TEN, (lambda k_, h_, pq_: lambda e: e.matmul(
                    pq_[0:1, :], oneq_bf[:], ysq[:, k_ % 2, h_ * 512:(h_ + 1) * 512],
                    start=(k_ == 0), stop=(k_ == NDC - 1)))(k, hf, pq),
                     waits=([(sS, sq_)] if hf == 0 else []),
                     incs=([(sT, 1)] if hf == 1 else []))
            st_evts[k] = (sV.n, sS.n, sT.n)
        mm2 = sT.n
        p.op(VEC, lambda e: e.tensor_copy(mu2_row[0:1, 0:512], ps[0][0:1, :]),
             waits=[(sT, mm2)], incs=[(sV, 1)])
        p.op(VEC, lambda e: e.tensor_copy(mu2_row[0:1, 512:1024], ps[5][0:1, :]), incs=[(sV, 1)])
        p.op(VEC, lambda e: e.tensor_mul(scr_row, mu2_row, mu2_row), incs=[(sV, 1)])
        p.op(VEC, lambda e: e.tensor_copy(var_row[0:1, 0:512], ps[1][0:1, :]), incs=[(sV, 1)])
        p.op(VEC, lambda e: e.tensor_copy(var_row[0:1, 512:1024], ps[6][0:1, :]), incs=[(sV, 1)])
        p.op(VEC, lambda e: e.tensor_sub(var_row, var_row, scr_row), incs=[(sV, 1)])
        rsqrt_nr(r2_row, var_row, scr_row, var_row)
        rows5 = sV.n
        for hh in range(2):
            p.op(TEN, (lambda h_: lambda e: e.matmul(
                ps[2 + h_][:], onerow_f[:], mu2_row[0:1, h_ * 512:(h_ + 1) * 512],
                start=True, stop=True))(hh),
                 waits=([(sV, rows5)] if hh == 0 else []), incs=[(sT, 1)])
            p.op(TEN, (lambda h_: lambda e: e.matmul(
                ps[4 if h_ == 0 else 7][:], onerow_f[:], r2_row[0:1, h_ * 512:(h_ + 1) * 512],
                start=True, stop=True))(hh), incs=[(sT, 1)])
        reps5_mm = sT.n

        nrm_evts = {}
        ff_cnt = [0, 0]
        for k in range(NDC):
            w = [(sT, reps5_mm)]
            if k >= 2:
                w.append((sV, nrm_evts[k - 2]))
                w.append((sff.sems[k % 2], 16 * ff_cnt[k % 2]))
            else:
                w += [(sV, st_evts[NDC - 2 + k][0]), (sS, st_evts[NDC - 2 + k][1])]
            ld = dma(SCAL, y_in[:, k % 2], y_scr[k], sy5.sems[k % 2], waits=w)
            ld2 = dma(SCAL, xT_st[:, k % 2], x_ownT[k * 128:(k + 1) * 128, :],
                      sxT.sems[k % 2], waits=(list(w) if k < 2 else []))
            for hh in range(2):
                p.op(VEC, (lambda k_, h_: lambda e: e.tensor_sub(
                    t_nrm[:, k_ % 2, h_ * 512:(h_ + 1) * 512],
                    y_in[:, k_ % 2, h_ * 512:(h_ + 1) * 512], ps[2 + h_][:]))(k, hh),
                     waits=([ld] if hh == 0 else []), incs=[(sV, 1)])
                p.op(VEC, (lambda k_, h_: lambda e: e.tensor_mul(
                    t_nrm[:, k_ % 2, h_ * 512:(h_ + 1) * 512],
                    t_nrm[:, k_ % 2, h_ * 512:(h_ + 1) * 512],
                    ps[4 if h_ == 0 else 7][:]))(k, hh), incs=[(sV, 1)])
            vv = sV.n
            w2 = [(sV, vv)]
            if k == 0:
                w2.append((sV, consts_done))
            if ff_cnt[k % 2] > 0:
                w2.append((sff.sems[k % 2], 16 * ff_cnt[k % 2]))
            p.op(VEC, (lambda k_: lambda e: e.tensor_scalar(
                y_in[:, k_ % 2], t_nrm[:, k_ % 2],
                gam_fm[:, k_:k_ + 1], bet_fm[:, k_:k_ + 1], OP.mult, OP.add))(k),
                 waits=w2, incs=[(sV, 1)])
            p.op(VEC, (lambda k_: lambda e: e.tensor_add(
                y_in[:, k_ % 2], y_in[:, k_ % 2], xT_st[:, k_ % 2]))(k),
                 waits=[ld2], incs=[(sV, 1)])
            nrm_evts[k] = sV.n
            dma(SCAL, xffn_scr[k], y_in[:, k % 2], sff.sems[k % 2], waits=[(sV, sV.n)])
            ff_cnt[k % 2] += 1
        xffn_all = sff.totals()
        nrm_last = sV.n

        if stop_after <= 5:
            return finalize()

        last_ten = sT.n
        for k in range(NDC):
            p.op(VEC, (lambda k_: lambda e: e.tensor_scalar_mul(
                xn_fm[:, k_], xn_fm[:, k_], cb_fm[:, k_:k_ + 1]))(k),
                 waits=([(sT, last_ten)] if k == 0 else []), incs=[(sV, 1)])
        qp_evt = sV.n
        p.op(GPS, lambda e: e.iota(diff_t, pattern=[[-1, 512]], base=0,
                                   channel_multiplier=1,
                                   allow_small_or_imprecise_dtypes=True),
             waits=[(sV, nrm_last)], incs=[(sG, 1)])
        p.op(GPS, lambda e: e.iota(thr_tab[:, 0:64], pattern=[[-128, 64]], base=0,
                                   channel_multiplier=0,
                                   allow_small_or_imprecise_dtypes=True), incs=[(sG, 1)])
        p.op(GPS, lambda e: e.iota(thr_tab[:, 64:128], pattern=[[-128, 64]], base=512,
                                   channel_multiplier=0,
                                   allow_small_or_imprecise_dtypes=True), incs=[(sG, 1)])
        p.op(GPS, lambda e: e.tensor_scalar_add(thr_tab[:], thr_tab[:], tb_col[:]),
             waits=[(sV, tb_done)], incs=[(sG, 1)])
        thr_done = sG.n

        cache = lambda st: big[:, st * 512:(st + 1) * 512]
        out_cnt = [0, 0, 0, 0]
        n_out = 0
        prev_B_last_mm = big_free_evt
        prev_evac = 0
        grp_guard = {}
        gidx = 0
        kb_cnt = [0, 0]
        prev_sig = {}
        prev_attn = {}

        for sp in range(npass * 4):
            pss, hd = sp // 4, sp % 4
            sig_evt = {}
            attn_evt = {}
            kp_evt = {}
            scores_mm_evt = {}
            for stp in range(0, NST, 2):
                blk = stp // 4
                if stp % 4 == 0:
                    ksl = blk % 2
                    w = list(xbf_all)
                    if sp == 0:
                        w += list(moe_all)
                    if blk >= 2:
                        w.append((sT, scores_mm_evt[4 * (blk - 1) - 1]))
                    if blk <= 1:
                        w.append((sT, prev_B_last_mm))
                        w.append((sV, qp_evt))
                        if prev_evac:
                            w.append((sV, prev_evac))
                    for j in range(8):
                        dmaT(SYNC, Kp[:, ksl, j],
                             xbf_scr[blk * 512:(blk + 1) * 512,
                                     (hd * 8 + j) * 128:(hd * 8 + j + 1) * 128],
                             skp.sems[ksl], waits=(w if j == 0 else []))
                        kb_cnt[ksl] += 1
                    kp_evt[blk] = (skp.sems[ksl], 16 * kb_cnt[ksl])
                pair = (stp, stp + 1)
                banks = (ps[stp % 2], ps[(stp + 1) % 2])
                for j in range(8):
                    for t in range(2):
                        st = pair[t]
                        w = []
                        if j == 0:
                            if t == 0:
                                w.append(kp_evt[blk])
                            if st >= 2:
                                w.append((sS, sig_evt[st - 2]))
                            elif sp > 0:
                                w.append((sS, prev_sig[NST - 2 + st]))
                        p.op(TEN, (lambda s_, j_, bk, h_, q_: lambda e: e.matmul(
                            bk[:], Kp[:, (s_ // 4) % 2, j_,
                                      (s_ % 4) * 128:((s_ % 4) + 1) * 128],
                            xn_fm[:, h_ * 8 + j_, q_ * 512:(q_ + 1) * 512],
                            start=(j_ == 0), stop=(j_ == 7)))(st, j, banks[t], hd, pss),
                             waits=w, incs=([(sT, 1)] if j == 7 else []))
                        if j == 7:
                            scores_mm_evt[st] = sT.n
                for t in range(2):
                    st = pair[t]
                    p.op(VEC, (lambda s_, q_: lambda e: e.tensor_scalar(
                        mask_t[:, s_ % 2], diff_t,
                        thr_tab[:, q_ * 64 + s_: q_ * 64 + s_ + 1], None,
                        OP.is_le))(st, pss),
                         waits=[(sG, thr_done)], incs=[(sV, 1)])
                    w = [(sT, scores_mm_evt[st])]
                    if st >= 2:
                        w.append((sV, attn_evt[st - 2]))
                    elif sp > 0:
                        w.append((sV, prev_attn[NST - 2 + st]))
                    p.op(SCAL, (lambda s_, bk: lambda e: e.activation(
                        sg_t[:, s_ % 2], bk[:], ACT.Sigmoid))(st, banks[t]),
                         waits=w, incs=[(sS, 1)])
                    sig_evt[st] = sS.n
                    w = [(sS, sig_evt[st])]
                    if st == 0:
                        w.append((sT, prev_B_last_mm))
                    p.op(VEC, (lambda s_: lambda e: e.tensor_mul(
                        cache(s_), sg_t[:, s_ % 2], mask_t[:, s_ % 2]))(st),
                         waits=w, incs=[(sV, 1)])
                    attn_evt[st] = sV.n

            rb_evts = {}
            for dsub in range(4):
                chunk = hd * 8 + dsub
                if dsub == 0:
                    w = list(xffn_all)
                    if prev_evac:
                        w.append((sV, prev_evac))
                else:
                    w = []
                rb_evts[dsub] = dma(SCAL, xffn_rb[:, dsub % 4],
                                    xffn_scr[chunk][:, pss * 512:(pss + 1) * 512],
                                    sfr.sems[dsub % 4], waits=w)
            grp_mm = {}
            for st in range(NST):
                sl = st % 2
                if gidx >= 2:
                    w = [grp_guard[gidx - 2]]
                else:
                    w = [(sT, big_free_evt)]
                vld = dma(SYNC if sl == 0 else GPS, v_t[:, sl],
                          xbf_scr[st * 128:(st + 1) * 128,
                                  hd * 1024:(hd + 1) * 1024], svl.sems[sl], waits=w)
                for dsub in range(8):
                    w2 = []
                    if dsub == 0:
                        w2.append(vld)
                        if st == 0:
                            w2.append((sV, attn_evt[NST - 1]))
                            w2.append((sS, sig_evt[NST - 1]))
                    p.op(TEN, (lambda s_, d_: lambda e: e.matmul(
                        ps[d_][:], v_t[:, s_ % 2, d_ * 128:(d_ + 1) * 128], cache(s_),
                        start=(s_ == 0), stop=(s_ == NST - 1)))(st, dsub),
                         waits=w2, incs=([(sT, 1)] if dsub == 7 else []))
                grp_mm[st] = sT.n
                grp_guard[gidx] = (sT, sT.n)
                gidx += 1
            for dsub in range(8):
                chunk = hd * 8 + dsub
                w = [(sT, grp_mm[NST - 1]), rb_evts[dsub]]
                sl = n_out % 4
                if out_cnt[sl] > 0:
                    w.append((sou.sems[sl], 16 * out_cnt[sl]))
                p.op(VEC, (lambda d_, c_, n_: lambda e: e.scalar_tensor_tensor(
                    out_st[:, n_ % 4], ps[d_][:], vb_fm[:, c_:c_ + 1],
                    xffn_rb[:, d_ % 4], OP.mult, OP.add))(dsub, chunk, n_out),
                     waits=w, incs=[(sV, 1)])
                vv = sV.n
                dma(SCAL, out_fm[chunk][:, pss * 512:(pss + 1) * 512],
                    out_st[:, n_out % 4], sou.sems[sl], waits=[(sV, vv)])
                out_cnt[sl] += 1
                n_out += 1
                if dsub + 4 < 8:
                    rb_evts[dsub + 4] = dma(
                        SCAL, xffn_rb[:, (dsub + 4) % 4],
                        xffn_scr[hd * 8 + dsub + 4][:, pss * 512:(pss + 1) * 512],
                        sfr.sems[(dsub + 4) % 4], waits=[(sV, vv)])
            prev_evac = sV.n
            prev_B_last_mm = grp_mm[NST - 1]
            prev_sig, prev_attn = sig_evt, attn_evt

        return finalize()
    return nc


_NC_CACHE = {}


def kernel(**inputs):
    x = np.asarray(inputs["L0_attn_x"], dtype=np.float32).reshape(T_ALL, D)
    if "nc" not in _NC_CACHE:
        _NC_CACHE["nc"] = build_program()
    nc = _NC_CACHE["nc"]

    common = {
        "x_full": np.ascontiguousarray(x),
        "up_w": np.ascontiguousarray(inputs["L0_ffn_up_w"], dtype=np.float32),
        "down_w": np.ascontiguousarray(inputs["L0_ffn_down_w"], dtype=np.float32),
        "fc1_w": np.ascontiguousarray(inputs["L0_ffn_fc1_w"], dtype=np.float32),
        "fc2_w": np.ascontiguousarray(inputs["L0_ffn_fc2_w"], dtype=np.float32),
        "exp_w": np.ascontiguousarray(inputs["L0_ffn_expert_w"], dtype=np.float32),
        "gate_wT": np.ascontiguousarray(np.asarray(inputs["L0_ffn_gate_w"], dtype=np.float32).T),
        "gamma": np.ascontiguousarray(np.asarray(inputs["L0_ffn_ln_gamma"], dtype=np.float32)),
        "beta": np.ascontiguousarray(np.asarray(inputs["L0_ffn_ln_beta"], dtype=np.float32)),
        "bv_q": np.ascontiguousarray(inputs["L0_attn_bv_q"], dtype=np.float32),
        "bv_k": np.ascontiguousarray(inputs["L0_attn_bv_k"], dtype=np.float32),
        "bv_v": np.ascontiguousarray(inputs["L0_attn_bv_v"], dtype=np.float32),
    }
    in_maps = []
    for c in range(N_CORES):
        shard = x[c * T_OWN:(c + 1) * T_OWN]
        m = dict(common)
        m["x_own"] = np.ascontiguousarray(shard)
        m["x_ownT"] = np.ascontiguousarray(shard.T)
        m["tb_in"] = np.array([[c * T_OWN]], dtype=np.float32)
        in_maps.append(m)

    res = bass_utils.run_bass_kernel_spmd(nc, in_maps, core_ids=list(range(N_CORES)))
    out = np.empty((T_ALL, D), dtype=np.float32)
    for c in range(N_CORES):
        o = np.asarray(res.results[c]["out_fm"])
        out[c * T_OWN:(c + 1) * T_OWN] = o.transpose(2, 0, 1).reshape(T_OWN, D)
    return out.reshape(1, T_ALL, D)
